# version 15
# baseline (speedup 1.0000x reference)
import numpy as np
from contextlib import ExitStack
import concourse.bass as bass
import concourse.mybir as mybir
import concourse.bass_utils as _bu
from concourse.bass_utils import run_bass_kernel_spmd

if not getattr(_bu, "_mk_walrus_patch", False):
    _orig_gwa = _bu.get_walrus_args

    def _gwa(*a, **kw):
        return _orig_gwa(*a, **kw) + ["--enable-dynamic-act-table"]
    _bu.get_walrus_args = _gwa
    _bu._mk_walrus_patch = True

F32 = mybir.dt.float32
BF16 = mybir.dt.bfloat16
AF = mybir.ActivationFunctionType
ALU = mybir.AluOpType
AX = mybir.AxisListType

NCORES = 8
D = 2048
NKC = 16
H = 8
HK = 128
HV = 256
KD = 1024
VD = 2048
CH = 64
EPS_LN = 1e-5
EPS_RMS = 1e-6
S16 = 1.0 / 16.0


class Ctr:
    def __init__(self, sem, step=1):
        self.sem, self.step, self.n = sem, step, 0

    def sig(self, instr, k=1):
        instr.then_inc(self.sem, self.step * k)
        self.n += self.step * k
        return self.n


class EW:
    def __init__(self, e, ctr):
        self.e, self.c = e, ctr

    def __getattr__(self, name):
        f = getattr(self.e, name)

        def g(*a, **k):
            r = f(*a, **k)
            if r is not None and hasattr(r, "then_inc"):
                self.c.sig(r)
            return r
        return g

    def wait(self, ctr, val=None):
        self.e.wait_ge(ctr.sem, ctr.n if val is None else val)


class MK:
    def __init__(self, cfg):
        self.cfg = cfg
        self.ntb = cfg["ntb"]
        self.ntok = 128 * self.ntb
        self.nch = self.ntok // CH
        self.E = cfg["E"]
        self.NFC = cfg["nfc"]
        self.FSH = cfg["fsh"]
        self.nc = bass.Bass("TRN2", target_bir_lowering=False)

    def TE(self, f):
        self.block.tensor(lambda e: f(EW(e, self.pe)))

    def VE(self, f):
        self.block.vector(lambda e: f(EW(e, self.ve)))

    def SE(self, f):
        self.block.scalar(lambda e: f(EW(e, self.se)))

    def SY(self, f):
        self.block.sync(lambda e: f(EW(e, self.sy)))

    def GP(self, f):
        self.block.gpsimd(lambda e: f(EW(e, self.gd)))

    def declare(self):
        nc = self.nc
        ntok, E, NFC, FSH = self.ntok, self.E, self.NFC, self.FSH
        inp = lambda n, s: nc.dram_tensor(n, s, F32, kind="ExternalInput")
        self.dx = inp("x", [ntok, D])
        self.dsel = inp("selmask", [128, NCORES])
        self.dwq = inp("wq", [KD // 128, 128, D])
        self.dwk = inp("wk", [KD // 128, 128, D])
        self.dwv = inp("wv", [VD // 128, 128, D])
        self.dwg = inp("wg", [VD // 128, 128, D])
        self.dgk1 = inp("gk1", [128, NKC * 16])
        self.dgk2 = inp("gk2", [16, KD])
        self.dgkb = inp("gkb", [128, KD // 128])
        self.dwo = inp("wo", [4, NKC // 4, 128, 2048])
        self.dn1w = inp("n1w", [128, NKC])
        self.dfnw = inp("fnw", [128, NKC])
        self.dgnw = inp("gnw", [128, 2])
        self.dwgate = inp("wgate", [NFC, 128, D])
        self.dwup = inp("wup", [NFC, 128, D])
        self.dwdn = inp("wdn", [4, NFC // 4, 128, 2048])
        self.drtw = inp("rtw", [128, NKC * E])
        self.debias = inp("ebias", [1, E])
        self.dsgw = inp("sgw", [FSH, 128, D])
        self.dsuw = inp("suw", [FSH, 128, D])
        self.dsdw = inp("sdw", [4, FSH // 4, 128, 2048])
        self.degu = inp("egu", [E, NKC, 128, D])
        self.ded = inp("ed", [E, 4, FSH // 4, 128, 2048])
        self.dout = nc.dram_tensor("out", [ntok, D], F32, kind="ExternalOutput")
        self.dspill = nc.dram_tensor("xnspill", [ntok, D], BF16, kind="Internal")
        self.CCR = H * HK + 8
        self.dccin = nc.dram_tensor("ccin", [self.CCR, HV], F32, kind="Internal")
        self.dccout = nc.dram_tensor("ccout", [NCORES * self.CCR, HV], F32,
                                     kind="Internal", addr_space="Shared")
        import ml_dtypes
        self.didf = nc.inline_tensor(np.eye(128, dtype=np.float32), "c_idf")
        self.didb = nc.inline_tensor(np.eye(128).astype(ml_dtypes.bfloat16), "c_idb")
        import ml_dtypes as _mld
        m = (np.arange(CH)[None, :] >= np.arange(CH)[:, None]).astype(np.float32)
        self.dmask = nc.inline_tensor(
            np.tile(np.concatenate([m, m], axis=0), (1, H)).astype(_mld.bfloat16),
            "c_mask")
        self.dones = nc.inline_tensor(np.ones((1, 128), dtype=np.float32), "c_ones")

    def alloc(self, st):
        nc, ntok = self.nc, self.ntok
        E = self.E
        sb = lambda n, s, d: st.enter_context(nc.sbuf_tensor(n, s, d))
        ps = lambda n, s: st.enter_context(nc.psum_tensor(n, s, F32))
        sem = lambda n: st.enter_context(nc.semaphore(n))
        self.RA = sb("RA", [128, NKC * ntok], BF16)
        self.RB = sb("RB", [128, NKC * ntok], BF16)
        self.RC = sb("RC", [128, NKC * ntok], BF16)
        self.RD = sb("RD", [128, NKC * ntok], BF16)
        self.RE = sb("RE", [128, max(NKC * ntok, 13312)], BF16)
        self.RF = sb("RF", [128, 10496], BF16)
        self.ring = [sb("rg%d" % i, [128, D], BF16) for i in range(5)]
        self.cid = sb("cid", [128, 128], F32)
        self.cidb = sb("cidb", [128, 128], BF16)
        self.cmask = sb("cmask", [128, H * CH], BF16)
        self.cones = sb("cones", [1, 128], F32)
        self.cn1w = sb("cn1w", [128, NKC], F32)
        self.cn1wq = sb("cn1wq", [128, NKC], F32)
        self.cfnw = sb("cfnw", [128, NKC], F32)
        self.cgnw = sb("cgnw", [128, 2], F32)
        self.cgkb = sb("cgkb", [128, KD // 128], F32)
        self.csel = sb("csel", [128, NCORES], F32)
        self.cebias = sb("cebias", [128, E], F32)
        self.cgk2 = sb("cgk2", [16, KD], BF16)
        self.rr2tm = sb("rr2tm", [128, self.ntb], F32)
        self.comb = sb("comb", [128, self.ntb * E], F32)
        self.vts = sb("vts", [128, 512], BF16)
        self.smf = sb("smf", [128, 64], F32)
        self.pA = ps("pA", [128, 1024])
        self.pB = ps("pB", [128, 1024])
        self.pC = ps("pC", [128, 1024])
        self.pD = ps("pD", [128, 1024])
        self.pe = Ctr(sem("s_pe"))
        self.ve = Ctr(sem("s_ve"))
        self.se = Ctr(sem("s_se"))
        self.sy = Ctr(sem("s_sy"), 16)
        self.gd = Ctr(sem("s_gd"), 16)
        self.cd = Ctr(sem("s_cd"), 16)
        self.cc = Ctr(sem("s_cc"))
        self.so = Ctr(sem("s_out"), 16)
        self.gsl = [Ctr(sem("s_gsl%d" % i), 16) for i in range(len(self.ring))]
        self.sp = Ctr(sem("s_sp"), 16)
        self.RBf = self.RB.bitcast(F32)
        RFf_ = self.RF.bitcast(F32)
        self.cgw = RFf_[:, 2048:2048 + NKC * E]
        self.glrt = RFf_[:, 2048 + NKC * E:2048 + NKC * E + self.ntb * 16]
        self.cgk1 = self.RE.bitcast(BF16)[:, 12288:12288 + NKC * 16] if False \
            else self.RE[:, 12288:12288 + NKC * 16]
        self.RDf = self.RD.bitcast(F32)
        self.cumbl = self.RDf[:, 4 * ntok:4 * ntok + H * self.nch]
        self.RFf = self.RF.bitcast(F32)
        self.REf = self.RE.bitcast(F32)
        self.pAb = self.pA.bitcast(BF16)
        self.pCb = self.pC.bitcast(BF16)
        self.pDb = self.pD.bitcast(BF16)
        self.lq = []
        self.lq_emitted = 0
        self.lq_ready = []
        self.ring_free = [None] * len(self.ring)
        self.ring_free_li = list(range(-len(self.ring), 0))

    def lq_push(self, src_ap, scale_cols=None, width=D):
        self.lq.append((src_ap, scale_cols, width, self.ve.n, self.sy.n))
        return len(self.lq) - 1

    def lq_emit_one(self):
        k = self.lq_emitted
        if self.ring_free_li[k % len(self.ring)] != k - len(self.ring):
            return False
        src_ap, scale_cols, width, ve_min, sy_min = self.lq[k]
        slot = self.ring[k % len(self.ring)]
        gate = self.ring_free[k % len(self.ring)]

        sctr = self.gsl[k % len(self.ring)]

        def j(g):
            if gate is not None:
                g.wait(gate[0], gate[1])
            dst = slot[:, :width]
            if src_ap.ndim == 3:
                dst = dst.rearrange("p (f w) -> p f w", w=src_ap.shape[2])
            sctr.sig(g.e.dma_start(out=dst, in_=src_ap))
        self.GP(j)
        rdy = (sctr, sctr.n)
        if scale_cols:
            def j2(v):
                v.wait(rdy[0], rdy[1])
                v.wait(self.ve, ve_min)
                v.e.wait_ge(self.sy.sem, sy_min)
                for colap, lo, hi in scale_cols:
                    v.tensor_scalar(slot[:, lo:hi], slot[:, lo:hi], colap,
                                    None, ALU.mult)
            self.VE(j2)
            rdy = (self.ve, self.ve.n)
        self.lq_ready.append(rdy)
        self.lq_emitted += 1
        return True

    def lq_need(self, li):
        while self.lq_emitted <= min(li, len(self.lq) - 1):
            if not self.lq_emit_one():
                break

    def w8(self, li):
        self.lq_need(li)
        assert li < self.lq_emitted, \
            "load %d blocked: slot %d not freed for %d" % (
                li, li % len(self.ring), li - len(self.ring))
        r = self.lq_ready[li]
        return (r[0], r[1], self.ring[li % len(self.ring)])

    def use(self, eng, li):
        assert li < self.lq_emitted, "load %d not emitted before body" % li
        eng.wait(*self.lq_ready[li])
        return self.ring[li % len(self.ring)]

    def free_slot(self, li, mark):
        self.ring_free[li % len(self.ring)] = mark
        self.ring_free_li[li % len(self.ring)] = li

    def done(self, li, prefetch=True):
        self.free_slot(li, (self.pe, self.pe.n))
        if prefetch:
            self.lq_need(li + len(self.ring))

    def n1cols(self, coltensor):
        return [(coltensor[:, c:c + 1], c * 128, (c + 1) * 128) for c in range(NKC)]

    def build(self):
        self.declare()
        with ExitStack() as st:
            self.block = st.enter_context(self.nc.Block())
            self.alloc(st)
            self.p0_consts()
            self.p1_input()
            self.p2_proj()
            self.p3_scan()
            self.p3b_cc_gate()
            self.p3c_corr()
            self.p3d_epilogue()
            self.p4_wo()
            self.p4b_ffn()
            self.p5_moe()
            self.p6_out()
        return self.nc

    def p0_consts(self):
        def jsy(s):
            for dst, src in [(self.cid, self.didf), (self.cidb, self.didb),
                             (self.cmask, self.dmask), (self.cones, self.dones),
                             (self.cgw, self.drtw), (self.cn1w, self.dn1w),
                             (self.cfnw, self.dfnw), (self.cgnw, self.dgnw),
                             (self.cgkb, self.dgkb), (self.csel, self.dsel)]:
                s.dma_start(out=dst[:], in_=src.ap())
            s.dma_start(out=self.cebias[0:1, :], in_=self.debias.ap())
        self.SY(jsy)
        self.c_rdy = self.sy.n

        def jg(g):
            g.dma_start(out=self.cgk1[:], in_=self.dgk1.ap())
            g.dma_start(out=self.cgk2[:], in_=self.dgk2.ap())
        self.GP(jg)
        self.smallw_rdy = self.gd.n

        def jv(v):
            v.wait(self.sy, self.c_rdy)
            v.tensor_scalar(self.cn1wq[:], self.cn1w[:], float(HK) ** -0.5,
                            None, ALU.mult)
        self.VE(jv)
        self.n1wq_rdy = self.ve.n

        def jt(t):
            t.wait(self.sy, self.c_rdy)
            t.matmul(self.pD[:, 0:self.E], self.cones[:], self.cebias[0:1, :],
                     start=True, stop=True)
        self.TE(jt)
        mmn = self.pe.n

        def jv2(v):
            v.wait(self.pe, mmn)
            v.tensor_copy(self.cebias[:], self.pD[:, 0:self.E])
        self.VE(jv2)
        self.consts_ve = self.ve.n

        self.li_wq = [self.lq_push(self.dwq.ap()[fb], self.n1cols(self.cn1wq))
                      for fb in range(KD // 128)]
        self.li_wk = [self.lq_push(self.dwk.ap()[fb], self.n1cols(self.cn1w))
                      for fb in range(KD // 128)]
        self.li_wv = [self.lq_push(self.dwv.ap()[fb], self.n1cols(self.cn1w))
                      for fb in range(VD // 128)]
        self.lq_need(len(self.ring) - 1)

    def p1_input(self):
        ntok, ntb = self.ntok, self.ntb
        xblk = self.REf[:, 0:2048]
        sqs = self.REf[:, 2048:4096]
        xnfmb = self.REf[:, 4096:6144]
        rr2row = self.RFf[0:1, 0:ntok]
        sm = self.smf
        self.spill_gd = {}
        prev_clear = None
        for tb in range(ntb):
            def jsy(s, tb=tb, prev_clear=prev_clear):
                if tb == 0:
                    s.wait(self.sy, self.c_rdy)
                else:
                    s.wait(self.ve, prev_clear)
                    s.wait(self.sp, self.spill_gd[tb - 1])
                s.dma_start(out=xblk, in_=self.dx.ap()[tb * 128:(tb + 1) * 128, :])
            self.SY(jsy)
            xld = self.sy.n

            def jse(s, tb=tb, xld=xld):
                s.wait(self.sy, xld)
                s.activation(sqs, xblk, AF.Square, accum_out=sm[:, 0:1])
            self.SE(jse)
            sqd = self.se.n

            def jve(v, sqd=sqd, xld=xld):
                v.wait(self.sy, xld)
                v.tensor_reduce(sm[:, 1:2], xblk, AX.X, ALU.add)
                v.wait(self.se, sqd)
                v.wait(self.ve)
                v.tensor_scalar(sm[:, 2:3], sm[:, 1:2], 1.0 / D, None, ALU.mult)
                v.tensor_scalar(sm[:, 3:4], sm[:, 0:1], 1.0 / D, None, ALU.mult)
                v.wait(self.ve)
                v.scalar_tensor_tensor(sm[:, 4:5], sm[:, 2:3], sm[:, 2:3],
                                       sm[:, 3:4], ALU.mult, ALU.subtract)
                v.wait(self.ve)
                v.tensor_scalar(sm[:, 4:5], sm[:, 4:5], -1.0, EPS_LN,
                                ALU.mult, op1=ALU.add)
                v.wait(self.ve)
                v.reciprocal(sm[:, 5:6], sm[:, 4:5])
            self.VE(jve)
            vr = self.ve.n

            def jse2(s, vr=vr):
                s.wait(self.ve, vr)
                s.activation(sm[:, 6:7], sm[:, 5:6], AF.Sqrt)
            self.SE(jse2)
            rstd = self.se.n

            def jve2(v, rstd=rstd):
                v.wait(self.se, rstd)
                v.tensor_scalar(xblk, xblk, sm[:, 2:3], sm[:, 6:7],
                                ALU.subtract, op1=ALU.mult)
            self.VE(jve2)
            xnd = self.ve.n

            def jgp(g, tb=tb, xnd=xnd):
                g.wait(self.ve, xnd)
                self.sp.sig(g.e.dma_start(
                    out=self.dspill.ap()[tb * 128:(tb + 1) * 128, :], in_=xblk))
            self.GP(jgp)
            self.spill_gd[tb] = self.sp.n

            def jse3(s, xnd=xnd):
                s.wait(self.ve, xnd)
                s.activation(sqs, xblk, AF.Square, accum_out=sm[:, 7:8])
            self.SE(jse3)
            sq2 = self.se.n

            def jve3(v, sq2=sq2):
                v.wait(self.se, sq2)
                v.tensor_scalar(sm[:, 8:9], sm[:, 7:8], 1.0 / D, EPS_RMS,
                                ALU.mult, op1=ALU.add)
                v.wait(self.ve)
                v.reciprocal(sm[:, 9:10], sm[:, 8:9])
            self.VE(jve3)
            vr3 = self.ve.n

            def jse4(s, tb=tb, vr3=vr3):
                s.wait(self.ve, vr3)
                s.activation(self.rr2tm[:, tb:tb + 1], sm[:, 9:10], AF.Sqrt)
            self.SE(jse4)
            rr2d = self.se.n

            def jte(t, tb=tb, xnd=xnd, rr2d=rr2d, prev_clear=prev_clear):
                t.wait(self.ve, xnd)
                if prev_clear is not None:
                    t.wait(self.ve, prev_clear)
                for c in range(NKC):
                    dst = (self.pA if c < 8 else self.pB)[:, (c % 8) * 128:(c % 8 + 1) * 128]
                    t.e.transpose(dst, xblk[:, c * 128:(c + 1) * 128], self.cid[:])
                t.wait(self.se, rr2d)
                t.transpose(self.pC[0:1, 0:128], self.rr2tm[:, tb:tb + 1], self.cid[:])
            self.TE(jte)
            trd = self.pe.n

            def jve4(v, tb=tb, trd=trd):
                v.wait(self.pe, trd)
                for c in range(NKC):
                    src = (self.pA if c < 8 else self.pB)[:, (c % 8) * 128:(c % 8 + 1) * 128]
                    v.tensor_copy(xnfmb[:, c * 128:(c + 1) * 128], src)
                    v.tensor_copy(self.RA[:, c * ntok + tb * 128:c * ntok + (tb + 1) * 128], src)
                v.tensor_copy(rr2row[0:1, tb * 128:(tb + 1) * 128], self.pC[0:1, 0:128])
            self.VE(jve4)
            xnfmd = self.ve.n

            def jte2(t, tb=tb, xnfmd=xnfmd):
                t.wait(self.ve, xnfmd)
                for c in range(NKC):
                    (t if c == NKC - 1 else t.e).matmul(
                        self.pD[:, 512:512 + self.E],
                        xnfmb[:, c * 128:(c + 1) * 128],
                        self.cgw[:, c * self.E:(c + 1) * self.E],
                        start=(c == 0), stop=(c == NKC - 1))
            self.TE(jte2)
            rlog = self.pe.n

            def jse5(s, tb=tb, rlog=rlog):
                s.wait(self.pe, rlog)
                if tb == 0:
                    s.wait(self.ve, self.consts_ve)
                s.activation(sm[:, 16:16 + self.E], self.pD[:, 512:512 + self.E],
                             AF.Sigmoid)
            self.SE(jse5)
            scd = self.se.n

            def jve6(v, tb=tb, scd=scd):
                E = self.E
                sc, bi, t1 = sm[:, 16:16 + E], sm[:, 32:32 + E], sm[:, 48:48 + E]
                v.wait(self.se, scd)
                v.tensor_tensor(bi, sc, self.cebias[:, 0:E], ALU.add)
                v.wait(self.ve)
                v.tensor_reduce(sm[:, 10:11], bi, AX.X, ALU.max)
                v.wait(self.ve)
                v.tensor_scalar(t1, bi, sm[:, 10:11], None, ALU.is_ge)
                v.wait(self.ve)
                v.scalar_tensor_tensor(bi, t1, -1e30, bi, ALU.mult, ALU.add)
                v.wait(self.ve)
                v.tensor_reduce(sm[:, 11:12], bi, AX.X, ALU.max)
                v.wait(self.ve)
                v.tensor_scalar(bi, bi, sm[:, 11:12], None, ALU.is_ge)
                v.wait(self.ve)
                v.tensor_tensor(t1, t1, bi, ALU.add)
                v.wait(self.ve)
                v.tensor_tensor(t1, t1, sc, ALU.mult)
                v.wait(self.ve)
                v.tensor_reduce(sm[:, 12:13], t1, AX.X, ALU.add)
                v.wait(self.ve)
                v.tensor_scalar(sm[:, 12:13], sm[:, 12:13], 1e-20, None, ALU.add)
                v.wait(self.ve)
                v.reciprocal(sm[:, 13:14], sm[:, 12:13])
                v.wait(self.ve)
                v.tensor_scalar(self.comb[:, tb * E:(tb + 1) * E], t1,
                                sm[:, 13:14], None, ALU.mult)
            self.VE(jve6)
            prev_clear = self.ve.n
        self.p1_ve = self.ve.n

    def p2_proj(self):
        ntok, ntb = self.ntok, self.ntb
        qe = self.RD[:, 0:H * ntok]
        kc = self.RD[:, H * ntok:2 * H * ntok]
        rr2b = self.RFf[:, 1024:1024 + ntok]
        nth = max(1, ntok // 512)
        tw = min(512, ntok)

        for th in range(nth):
            def jt(t, th=th):
                t.wait(self.ve, self.p1_ve)
                if th > 0:
                    t.wait(self.ve, self.rrb_ev)
                t.matmul(self.pD[:, 0:tw], self.cones[:],
                         self.RFf[0:1, th * 512:th * 512 + tw], start=True, stop=True)
            self.TE(jt)
            mmn = self.pe.n

            def jv(v, th=th, mmn=mmn):
                v.wait(self.pe, mmn)
                v.tensor_copy(rr2b[:, th * 512:th * 512 + tw], self.pD[:, 0:tw])
            self.VE(jv)
            self.rrb_ev = self.ve.n
        rrb_rdy = self.ve.n

        psq = [self.pA[:, 0:512], self.pA[:, 512:1024],
               self.pB[:, 0:512], self.pB[:, 512:1024]]
        prev_ev = {}
        tile = 0
        for lis, dst in [(self.li_wq, qe), (self.li_wk, kc)]:
            for fb in range(H):
                for th in range(nth):
                    bank = psq[tile % 4]
                    key = tile % 4
                    tile += 1
                    rdy = self.w8(lis[fb])

                    def jt(t, fb=fb, th=th, lis=lis, bank=bank, key=key, rdy=rdy):
                        t.wait(rdy[0], rdy[1])
                        w = rdy[2]
                        if key in prev_ev:
                            t.wait(self.ve, prev_ev[key])
                        for c in range(NKC):
                            (t if c == NKC - 1 else t.e).matmul(
                                bank[:, 0:tw], w[:, c * 128:(c + 1) * 128],
                                self.xn_rhs(c, th), start=(c == 0),
                                stop=(c == NKC - 1))
                    self.TE(jt)
                    mmn = self.pe.n
                    if th == nth - 1:
                        self.done(lis[fb])

                    def jv(v, fb=fb, th=th, dst=dst, bank=bank, mmn=mmn):
                        v.wait(self.pe, mmn)
                        v.wait(self.ve, rrb_rdy)
                        v.tensor_tensor(
                            dst[:, fb * ntok + th * 512:fb * ntok + th * 512 + tw],
                            bank[:, 0:tw], rr2b[:, th * 512:th * 512 + tw], ALU.mult)
                    self.VE(jv)
                    prev_ev[key] = self.ve.n

        def jv_gk1(v):
            v.wait(self.gd, self.smallw_rdy)
            v.wait(self.sy, self.c_rdy)
            for c in range(NKC):
                v.tensor_scalar(self.cgk1[:, c * 16:(c + 1) * 16],
                                self.cgk1[:, c * 16:(c + 1) * 16],
                                self.cn1w[:, c:c + 1], None, ALU.mult)
        self.VE(jv_gk1)
        gk1s = self.ve.n

        glr_ev = {}
        for tb in range(ntb):
            def jt2(t, tb=tb):
                t.wait(self.ve, gk1s)
                if tb > 0:
                    t.wait(self.ve, glr_ev[tb - 1])
                for c in range(NKC):
                    (t if c == NKC - 1 else t.e).matmul(
                        self.pD[:, 512:528],
                        self.RA[:, c * ntok + tb * 128:c * ntok + (tb + 1) * 128],
                        self.cgk1[:, c * 16:(c + 1) * 16],
                        start=(c == 0), stop=(c == NKC - 1))
            self.TE(jt2)
            mmn = self.pe.n

            def jv2(v, tb=tb, mmn=mmn):
                v.wait(self.pe, mmn)
                v.tensor_scalar(self.glrt[:, tb * 16:(tb + 1) * 16],
                                self.pD[:, 512:528], self.rr2tm[:, tb:tb + 1],
                                None, ALU.mult)
            self.VE(jv2)
            glr_ev[tb] = self.ve.n
        glrt_rdy = self.ve.n

        for tb in range(ntb):
            sl = (tb % 2) * 512

            def jt4(t, tb=tb, sl=sl):
                t.wait(self.ve, glrt_rdy)
                if tb >= 2:
                    t.wait(self.ve, glr_ev[("t", tb - 2)])
                t.transpose(self.pD[0:16, sl:sl + 128],
                            self.glrt[:, tb * 16:(tb + 1) * 16], self.cid[:])
            self.TE(jt4)
            mmn = self.pe.n

            def jv4(v, tb=tb, sl=sl, mmn=mmn):
                v.wait(self.pe, mmn)
                v.tensor_copy(self.RF[0:16, 8448 + tb * 128:8448 + (tb + 1) * 128],
                              self.pD[0:16, sl:sl + 128])
            self.VE(jv4)
            glr_ev[("t", tb)] = self.ve.n
        glrf_rdy = self.ve.n

        scr = [(self.RFf[:, 2048:2560], self.RFf[:, 2560:3072], self.RFf[:, 3072:3584]),
               (self.RFf[:, 0:512], self.RFf[:, 512:1024], self.RFf[:, 3584:4096])]
        gb = [self.pD[:, 0:512], self.pD[:, 512:1024]]
        chain_hist = {}
        gjobs = [(kb, th) for kb in range(KD // 128) for th in range(nth)]
        vjobs = [(fb, th) for fb in range(VD // 128) for th in range(nth)]
        gi = 0

        def chain_tile(k):
            kb, th = gjobs[k]
            s1, s2, s3 = scr[k % 2]
            bank = gb[k % 2][:, 0:tw]

            def jt5(t, kb=kb, th=th, bank=bank, k=k):
                t.wait(self.ve, glrf_rdy)
                if k >= 2:
                    t.wait(self.ve, chain_hist[k - 2])
                t.matmul(bank, self.cgk2[:, kb * 128:(kb + 1) * 128],
                         self.RF[0:16, 8448 + th * 512:8448 + th * 512 + tw],
                         start=True, stop=True)
            self.TE(jt5)
            mmn = self.pe.n

            def jse(s, kb=kb, bank=bank, mmn=mmn, s1=s1, s2=s2):
                s.wait(self.pe, mmn)
                s.activation(s1[:, 0:tw], bank, AF.Identity,
                             bias=self.cgkb[:, kb:kb + 1])
                s.wait(self.se)
                s.activation(s2[:, 0:tw], s1[:, 0:tw], AF.Relu, scale=-1.0)
            self.SE(jse)
            sen = self.se.n

            def jve_(v, sen=sen, s1=s1, s2=s2, s3=s3):
                v.wait(self.se, sen)
                v.scalar_tensor_tensor(s3[:, 0:tw], s2[:, 0:tw], 2.0,
                                       s1[:, 0:tw], ALU.mult, ALU.add)
            self.VE(jve_)
            ven = self.ve.n

            def jse2(s, ven=ven, s3=s3):
                s.wait(self.ve, ven)
                s.activation(s3[:, 0:tw], s3[:, 0:tw], AF.Exp, scale=-1.0)
                s.wait(self.se)
                s.activation(s3[:, 0:tw], s3[:, 0:tw], AF.Ln, bias=1.0)
            self.SE(jse2)
            sen2 = self.se.n

            def jve2(v, kb=kb, th=th, sen2=sen2, s2=s2, s3=s3):
                v.wait(self.se, sen2)
                v.tensor_tensor(s2[:, 0:tw], s2[:, 0:tw], s3[:, 0:tw], ALU.add)
                v.wait(self.ve)
                for cc in range(tw // CH):
                    tok0 = th * 512 + cc * CH
                    v.tensor_tensor_scan(
                        self.RBf[:, kb * ntok + tok0:kb * ntok + tok0 + CH],
                        s2[:, cc * CH:(cc + 1) * CH],
                        s2[:, cc * CH:(cc + 1) * CH], 0.0, ALU.add, ALU.bypass)
            self.VE(jve2)
            chain_hist[k] = self.ve.n

        for vi, (fb, th) in enumerate(vjobs):
            bank = psq[tile % 4]
            key = tile % 4
            tile += 1
            rdy = self.w8(self.li_wv[fb])

            def jtv(t, fb=fb, th=th, bank=bank, key=key, rdy=rdy):
                t.wait(rdy[0], rdy[1])
                w = rdy[2]
                if key in prev_ev:
                    t.wait(self.ve, prev_ev[key])
                for c in range(NKC):
                    (t if c == NKC - 1 else t.e).matmul(
                        bank[:, 0:tw], w[:, c * 128:(c + 1) * 128],
                        self.xn_rhs(c, th), start=(c == 0),
                        stop=(c == NKC - 1))
            self.TE(jtv)
            mmn = self.pe.n
            if th == nth - 1:
                self.done(self.li_wv[fb])

            def jvv(v, th=th, bank=bank, mmn=mmn):
                v.wait(self.pe, mmn)
                v.wait(self.ve, rrb_rdy)
                if "vt_pe" in prev_ev:
                    v.wait(self.pe, prev_ev["vt_pe"])
                v.tensor_tensor(self.vts[:, 0:tw], bank[:, 0:tw],
                                rr2b[:, th * 512:th * 512 + tw], ALU.mult)
            self.VE(jvv)
            evn = self.ve.n

            def jtt(t, evn=evn):
                t.wait(self.ve, evn)
                if "vt_ve" in prev_ev:
                    t.wait(self.ve, prev_ev["vt_ve"])
                for tt in range(tw // 128):
                    (t if tt == tw // 128 - 1 else t.e).transpose(
                        self.pCb[:, tt * 128:(tt + 1) * 128],
                        self.vts[:, tt * 128:(tt + 1) * 128],
                        self.cidb[:])
            self.TE(jtt)
            prev_ev["vt_pe"] = self.pe.n
            ttn = self.pe.n

            def jvt(v, fb=fb, th=th, ttn=ttn):
                v.wait(self.pe, ttn)
                for tt in range(tw // 128):
                    tb = (th * 512 + tt * 128) // 128
                    v.tensor_copy(
                        self.RC[:, tb * 2048 + fb * 128:tb * 2048 + (fb + 1) * 128],
                        self.pCb[:, tt * 128:(tt + 1) * 128])
            self.VE(jvt)
            prev_ev["vt_ve"] = self.ve.n
            prev_ev[key] = self.ve.n

            while gi < len(gjobs) and gi * len(vjobs) < (vi + 1) * len(gjobs):
                chain_tile(gi)
                gi += 1
        while gi < len(gjobs):
            chain_tile(gi)
            gi += 1
        self.p2_ve = self.ve.n
        self.p2_se = self.se.n

    def xn_rhs(self, c, th):
        return self.RA[:, c * self.ntok + th * 512:c * self.ntok + th * 512 + min(512, self.ntok)]

    def p3_scan(self):
        ntok, nch = self.ntok, self.nch
        qe = self.RD[:, 0:H * ntok]
        kc = self.RD[:, H * ntok:2 * H * ntok]
        S = self.RFf[:, 0:2048]
        ep = self.RFf[:, 2048:2560]
        en = self.RFf[:, 2560:3072]
        blsc = self.RFf[:, 4096:4104]
        expbl = self.RFf[:, 4104:4112]
        kd = self.RF[:, 6144:6656]
        kutm = self.RF[:, 6656:7680]
        ATs = self.RF[:, 7680:8192]
        Sbf = self.RF[:, 8448:10496]
        obuf = self.RE
        cbh = self.RBf[:, 0:H * ntok].rearrange("p (h t) -> p h t", h=H)

        def jvS(v):
            v.wait(self.ve, self.p2_ve)
            v.e.wait_ge(self.sp.sem, self.spill_gd[self.ntb - 1])
            v.memset(S, 0.0)
            v.memset(Sbf, 0.0)
        self.VE(jvS)
        S_rdy = self.ve.n

        oev, Supd, ATev = {}, {}, {}
        for c in range(nch):
            tb, half = c // 2, c % 2
            prow = 64 * half
            tok0 = c * CH
            cb3 = cbh[:, :, tok0:tok0 + CH]

            def jv1(v, c=c):
                if c == 0:
                    v.wait(self.ve, self.p2_ve)
                    v.wait(self.se, self.p2_se)
                else:
                    v.wait(self.se, ATev[("se", c - 1)])
                v.tensor_scalar(blsc,
                                cbh[:, :, tok0 + CH - 1:tok0 + CH]
                                .rearrange("p h o -> p (h o)"),
                                -S16, None, ALU.mult)
            self.VE(jv1)
            v1 = self.ve.n

            def js1(s, c=c, v1=v1, cb3=cb3):
                s.wait(self.ve, v1)
                s.activation(expbl, blsc, AF.Exp)
                s.activation(ep.rearrange("p (h t) -> p h t", h=H), cb3,
                             AF.Exp, scale=-S16)
                s.activation(en.rearrange("p (h t) -> p h t", h=H), cb3,
                             AF.Exp, scale=S16)
            self.SE(js1)
            s1n = self.se.n
            ATev[("se", c)] = s1n

            qe3 = qe.rearrange("p (h t) -> p h t", h=H)[:, :, tok0:tok0 + CH]
            kc3 = kc.rearrange("p (h t) -> p h t", h=H)[:, :, tok0:tok0 + CH]

            def jv2(v, c=c, qe3=qe3, kc3=kc3, s1n=s1n):
                v.wait(self.se, s1n)
                if c >= 1:
                    v.wait(self.pe, ATev[("pe", c - 1)])
                v.tensor_tensor(qe3, qe3, ep.rearrange("p (h t) -> p h t", h=H),
                                ALU.mult)
                v.tensor_tensor(kd.rearrange("p (h t) -> p h t", h=H), kc3,
                                en.rearrange("p (h t) -> p h t", h=H), ALU.mult)
            self.VE(jv2)
            v2 = self.ve.n

            def jt1(t, c=c, prow=prow, tok0=tok0, v2=v2):
                t.wait(self.ve, v2)
                if c >= 1:
                    t.wait(self.ve, ATev[c - 1])
                for h in range(H):
                    (t if h == H - 1 else t.e).matmul(
                        self.pC[prow:prow + 64, h * 64:(h + 1) * 64],
                        kd[:, h * 64:(h + 1) * 64],
                        qe[:, h * ntok + tok0:h * ntok + tok0 + CH],
                        start=(h == 0), stop=(h == H - 1))
            self.TE(jt1)
            t1n = self.pe.n

            def jv3(v, c=c, prow=prow, tok0=tok0, t1n=t1n):
                v.wait(self.pe, t1n)
                v.tensor_copy(self.RBf[:, tok0:tok0 + H],
                              cbh[:, :, tok0 + CH - 1:tok0 + CH]
                              .rearrange("p h o -> p (h o)"))
                v.wait(self.ve)
                v.tensor_tensor(ATs[prow:prow + 64, :],
                                self.pC[prow:prow + 64, 0:H * CH],
                                self.cmask[prow:prow + 64, :], ALU.mult)
                for h in range(H):
                    v.tensor_scalar(kd[:, h * 64:(h + 1) * 64],
                                    kd[:, h * 64:(h + 1) * 64],
                                    expbl[:, h:h + 1], None, ALU.mult)
            self.VE(jv3)
            ATev[c] = self.ve.n
            v3 = self.ve.n

            def jt2(t, c=c, prow=prow, v3=v3):
                t.wait(self.ve, v3)
                for h in range(H):
                    (t if h == H - 1 else t.e).transpose(
                        self.pCb[prow:prow + 64, 1024 + h * 128:1024 + (h + 1) * 128],
                        kd[:, h * 64:(h + 1) * 64], self.cidb[:])
            self.TE(jt2)
            t2n = self.pe.n
            ATev[("pe", c)] = t2n

            def jv4(v, c=c, prow=prow, t2n=t2n):
                v.wait(self.pe, t2n)
                v.tensor_copy(kutm[prow:prow + 64, :],
                              self.pCb[prow:prow + 64, 1024:2048])
            self.VE(jv4)
            kutd = self.ve.n

            def jt3(t, c=c, tb=tb, prow=prow, tok0=tok0, v2=v2):
                t.wait(self.ve, v2)
                t.wait(self.ve, S_rdy if c == 0 else Supd[c - 1])
                if c >= 1:
                    t.wait(self.ve, oev[c - 1])
                for h in range(H):
                    po = (self.pA if h < 4 else self.pB)[prow:prow + 64,
                                                         (h % 4) * 256:(h % 4 + 1) * 256]
                    t.e.matmul(po, qe[:, h * ntok + tok0:h * ntok + tok0 + CH],
                               Sbf[:, h * 256:(h + 1) * 256],
                               start=(h % 2 == 0), stop=False)
                t.wait(self.ve, ATev[c])
                for h in range(H):
                    po = (self.pA if h < 4 else self.pB)[prow:prow + 64,
                                                         (h % 4) * 256:(h % 4 + 1) * 256]
                    (t if h == H - 1 else t.e).matmul(
                        po, ATs[prow:prow + 64, h * 64:(h + 1) * 64],
                        self.RC[prow:prow + 64, tb * 2048 + h * 256:tb * 2048 + (h + 1) * 256],
                        start=False, stop=(h % 2 == 1))
            self.TE(jt3)
            t3n = self.pe.n

            def jv5(v, c=c, tb=tb, prow=prow, t3n=t3n):
                v.wait(self.pe, t3n)
                v.tensor_copy(obuf[prow:prow + 64, tb * 2048:tb * 2048 + 1024],
                              self.pA[prow:prow + 64, 0:1024])
                v.tensor_copy(obuf[prow:prow + 64, tb * 2048 + 1024:(tb + 1) * 2048],
                              self.pB[prow:prow + 64, 0:1024])
            self.VE(jv5)
            oev[c] = self.ve.n

            for rnd in range(2):
                def jt4(t, c=c, tb=tb, prow=prow, rnd=rnd, kutd=kutd):
                    t.wait(self.ve, kutd)
                    if rnd == 0 and c >= 1:
                        t.wait(self.ve, Supd[c - 1])
                    if rnd == 1:
                        t.wait(self.ve, Supd[(c, 0)])
                    for hh in range(4):
                        h = rnd * 4 + hh
                        (t if hh == 3 else t.e).matmul(
                            self.pD[:, hh * 256:(hh + 1) * 256],
                            kutm[prow:prow + 64, h * 128:(h + 1) * 128],
                            self.RC[prow:prow + 64, tb * 2048 + h * 256:tb * 2048 + (h + 1) * 256],
                            start=(hh % 2 == 0), stop=(hh % 2 == 1))
                self.TE(jt4)
                t4n = self.pe.n

                def jv6(v, c=c, rnd=rnd, t4n=t4n):
                    v.wait(self.pe, t4n)
                    v.wait(self.ve, oev[c])
                    for hh in range(4):
                        h = rnd * 4 + hh
                        v.scalar_tensor_tensor(S[:, h * 256:(h + 1) * 256],
                                               S[:, h * 256:(h + 1) * 256],
                                               expbl[:, h:h + 1],
                                               self.pD[:, hh * 256:(hh + 1) * 256],
                                               ALU.mult, ALU.add)
                self.VE(jv6)
                if rnd == 0:
                    Supd[(c, 0)] = self.ve.n
                else:
                    vcp = self.ve.n

                    def jv7(v, vcp=vcp):
                        v.wait(self.ve, vcp)
                        v.tensor_copy(Sbf, S)
                    self.VE(jv7)
                    Supd[c] = self.ve.n
        self.p3_ve = self.ve.n
        self.scan_Supd_last = Supd[nch - 1]

    def p3b_cc_gate(self):
        ntok, ntb, nch = self.ntok, self.ntb, self.nch
        S = self.RFf[:, 0:2048]
        bs = self.RFf[:, 4112:4120]
        bsr = self.RFf[0:8, 4416:4672]
        cumh = self.cumbl[:, 0:H * nch].rearrange("p (h cc) -> p h cc", h=H)

        def jv(v):
            v.wait(self.ve, self.p3_ve)
            blbs = self.RBf[:, 0:nch * CH].rearrange("p (cc w) -> p w cc", w=CH)
            for h in range(H):
                v.tensor_tensor_scan(self.cumbl[:, h * nch:(h + 1) * nch],
                                     blbs[:, h, :], blbs[:, h, :],
                                     0.0, ALU.add, ALU.bypass)
            v.wait(self.ve)
            v.tensor_copy(bs, cumh[:, :, nch - 1:nch].rearrange("p h o -> p (h o)"))
        self.VE(jv)
        bs_rdy = self.ve.n

        def jt(t):
            t.wait(self.ve, bs_rdy)
            t.transpose(self.pD[0:8, 0:128], bs, self.cid[:])
        self.TE(jt)
        bst = self.pe.n

        def jv3(v):
            v.wait(self.pe, bst)
            v.memset(bsr, 0.0)
            v.wait(self.ve)
            v.tensor_copy(bsr[0:8, 0:128], self.pD[0:8, 0:128])
        self.VE(jv3)
        bsr_rdy = self.ve.n

        def jg(g):
            g.e.wait_ge(self.ve.sem, bsr_rdy)
            for h in range(H):
                self.cd.sig(g.e.dma_start(
                    out=self.dccin.ap()[h * 128:(h + 1) * 128, :],
                    in_=S[:, h * 256:(h + 1) * 256]))
            self.cd.sig(g.e.dma_start(
                out=self.dccin.ap()[H * 128:H * 128 + 8, :], in_=bsr))
            g.e.wait_ge(self.cd.sem, self.cd.n)
            self.cc.sig(g.e.collective_compute(
                "AllGather", ALU.bypass, replica_groups=[list(range(NCORES))],
                ins=[self.dccin.ap()], outs=[self.dccout.ap()]))
        self.GP(jg)
        cc_done = self.cc.n

        rrg = self.RFf[:, 2048:2048 + ntok]
        rrr = self.RFf[0:1, 1024:1024 + ntok]
        rr_ev = {}
        stag_cd = self.cd.n
        for tb in range(ntb):
            def jtr(t, tb=tb):
                t.wait(self.ve, bsr_rdy)
                if tb == 0:
                    t.e.wait_ge(self.cd.sem, stag_cd)
                if tb >= 2:
                    t.wait(self.ve, rr_ev[tb - 2])
                t.transpose(self.pD[0:1, (tb % 2) * 512:(tb % 2) * 512 + 128],
                            self.rr2tm[:, tb:tb + 1], self.cid[:])
            self.TE(jtr)
            n = self.pe.n

            def jvr(v, tb=tb, n=n):
                v.wait(self.pe, n)
                v.tensor_copy(rrr[0:1, tb * 128:(tb + 1) * 128],
                              self.pD[0:1, (tb % 2) * 512:(tb % 2) * 512 + 128])
            self.VE(jvr)
            rr_ev[tb] = self.ve.n
        rrr_rdy = self.ve.n

        nth = max(1, ntok // 512)
        tw = min(512, ntok)
        for th in range(nth):
            def jtb(t, th=th):
                t.wait(self.ve, rrr_rdy)
                if ("rb", th - 1) in rr_ev:
                    t.wait(self.ve, rr_ev[("rb", th - 1)])
                t.matmul(self.pD[:, 0:tw], self.cones[:],
                         rrr[0:1, th * 512:th * 512 + tw], start=True, stop=True)
            self.TE(jtb)
            bn = self.pe.n

            def jvb(v, th=th, bn=bn):
                v.wait(self.pe, bn)
                v.tensor_copy(rrg[:, th * 512:th * 512 + tw], self.pD[:, 0:tw])
            self.VE(jvb)
            rr_ev[("rb", th)] = self.ve.n
        rrg_rdy = self.ve.n

        self.li_wg = [self.lq_push(self.dwg.ap()[fb], self.n1cols(self.cn1w))
                      for fb in range(VD // 128)]
        psq = [self.pA[:, 0:512], self.pA[:, 512:1024],
               self.pB[:, 0:512], self.pB[:, 512:1024]]
        prev = {}
        tile = 0
        for fb in range(VD // 128):
            for th in range(nth):
                bank = psq[tile % 4]
                key = tile % 4
                tile += 1

                rdy = self.w8(self.li_wg[fb])

                def jtg(t, fb=fb, th=th, bank=bank, key=key, rdy=rdy):
                    t.wait(rdy[0], rdy[1])
                    w = rdy[2]
                    if key in prev:
                        t.wait(self.ve, prev[key])
                    for c in range(NKC):
                        (t if c == NKC - 1 else t.e).matmul(
                            bank[:, 0:tw], w[:, c * 128:(c + 1) * 128],
                            self.xn_rhs(c, th), start=(c == 0),
                            stop=(c == NKC - 1))
                self.TE(jtg)
                mmn = self.pe.n
                if th == nth - 1:
                    self.done(self.li_wg[fb])

                def jvg(v, th=th, bank=bank, mmn=mmn):
                    v.wait(self.pe, mmn)
                    v.wait(self.ve, rrg_rdy)
                    if "gt_pe" in prev:
                        v.wait(self.pe, prev["gt_pe"])
                    v.tensor_tensor(self.vts[:, 0:tw], bank[:, 0:tw],
                                    rrg[:, th * 512:th * 512 + tw], ALU.mult)
                self.VE(jvg)
                evn = self.ve.n

                def jtgt(t, evn=evn):
                    t.wait(self.ve, evn)
                    if "gt_ve" in prev:
                        t.wait(self.ve, prev["gt_ve"])
                    for tt in range(tw // 128):
                        (t if tt == tw // 128 - 1 else t.e).transpose(
                            self.pCb[:, tt * 128:(tt + 1) * 128],
                            self.vts[:, tt * 128:(tt + 1) * 128],
                            self.cidb[:])
                self.TE(jtgt)
                prev["gt_pe"] = self.pe.n
                ttn = self.pe.n

                def jvgt(v, fb=fb, th=th, ttn=ttn):
                    v.wait(self.pe, ttn)
                    for tt in range(tw // 128):
                        tb = (th * 512 + tt * 128) // 128
                        v.tensor_copy(
                            self.RB[:, tb * 2048 + fb * 128:tb * 2048 + (fb + 1) * 128],
                            self.pCb[:, tt * 128:(tt + 1) * 128])
                self.VE(jvgt)
                prev["gt_ve"] = self.ve.n
                prev[key] = self.ve.n
        self.gate_ve = self.ve.n

        SJ = self.RFf[:, 2048:4096]
        SJ3 = SJ.rearrange("p (h v) -> p h v", h=H)
        bacc = self.RFf[:, 4120:4128]
        ebac = self.RFf[:, 4128:4136]
        bsj = self.RFf[:, 4136:4144]

        def jvz(v):
            v.wait(self.cc, cc_done)
            v.wait(self.ve, self.gate_ve)
            v.memset(S, 0.0)
            v.memset(bacc, 0.0)
        self.VE(jvz)
        zn = self.ve.n

        sin_ev = {}
        for jj in range(NCORES - 1, -1, -1):
            base = jj * self.CCR

            def jgl(g, jj=jj, base=base):
                g.e.wait_ge(self.ve.sem, zn if jj == NCORES - 1 else sin_ev[jj + 1])
                self.cd.sig(g.e.dma_start(
                    out=SJ3,
                    in_=self.dccout.ap()[base:base + H * 128, :].rearrange(
                        "(h k) v -> k h v", h=H)))
                self.cd.sig(g.e.dma_start(
                    out=bsj,
                    in_=self.dccout.ap()[base + H * 128:base + H * 128 + 8, 0:128]
                    .rearrange("h k -> k h"), allow_slow_non_contiguous=True))
            self.GP(jgl)
            ld = self.cd.n

            def jse(s, ld=ld):
                s.e.wait_ge(self.cd.sem, ld)
                s.activation(ebac, bacc, AF.Exp, scale=-S16)
            self.SE(jse)
            sen = self.se.n

            def jvj(v, jj=jj, sen=sen, ld=ld):
                v.wait(self.se, sen)
                v.e.wait_ge(self.cd.sem, ld)
                v.tensor_scalar(ebac, ebac, self.csel[:, jj:jj + 1], None, ALU.mult)
                v.wait(self.ve)
                for h in range(H):
                    v.scalar_tensor_tensor(S[:, h * 256:(h + 1) * 256],
                                           SJ[:, h * 256:(h + 1) * 256],
                                           ebac[:, h:h + 1],
                                           S[:, h * 256:(h + 1) * 256],
                                           ALU.mult, ALU.add)
                v.scalar_tensor_tensor(bacc, bsj, self.csel[:, jj:jj + 1],
                                       bacc, ALU.mult, ALU.add)
            self.VE(jvj)
            sin_ev[jj] = self.ve.n

        def jvcp(v):
            v.wait(self.ve, sin_ev[0])
            v.tensor_copy(self.RF[:, 8448:10496], S)
        self.VE(jvcp)
        self.sin_ve = self.ve.n

    def p3c_corr(self):
        ntok, nch = self.ntok, self.nch
        qe = self.RD[:, 0:H * ntok]
        S = self.RF[:, 8448:10496]
        eB = self.RFf[:, 4144:4152]
        q2 = self.RF[:, 6144:6656]
        obuf = self.RE
        cumh = self.cumbl[:, 0:H * nch].rearrange("p (h cc) -> p h cc", h=H)
        corr_mm, corr_ev, corr_q2 = {}, {}, {}
        for c in range(nch):
            tb, half = c // 2, c % 2
            prow = 64 * half
            tok0 = c * CH

            def jse(s, c=c):
                if c == 0:
                    s.wait(self.ve, self.sin_ve)
                else:
                    s.wait(self.ve, corr_q2[c - 1])
                    s.activation(eB, cumh[:, :, c - 1:c].rearrange("p h o -> p (h o)"),
                                 AF.Exp, scale=-S16)
            self.SE(jse)
            sen = self.se.n

            def jv(v, c=c, tok0=tok0, sen=sen):
                v.wait(self.se, sen)
                if c == 0:
                    v.wait(self.ve, self.sin_ve)
                    v.memset(eB, 1.0)
                    v.wait(self.ve)
                if c >= 1:
                    v.wait(self.pe, corr_mm[c - 1])
                for h in range(H):
                    v.tensor_scalar(q2[:, h * 64:(h + 1) * 64],
                                    qe[:, h * ntok + tok0:h * ntok + tok0 + CH],
                                    eB[:, h:h + 1], None, ALU.mult)
            self.VE(jv)
            vn = self.ve.n
            corr_q2[c] = vn

            pa, pb = (self.pA, self.pB) if c % 2 == 0 else (self.pC, self.pD)

            def jt(t, c=c, prow=prow, vn=vn, pa=pa, pb=pb):
                t.wait(self.ve, vn)
                if c >= 2:
                    t.wait(self.ve, corr_ev[c - 2])
                for h in range(H):
                    po = (pa if h < 4 else pb)[prow:prow + 64,
                                               (h % 4) * 256:(h % 4 + 1) * 256]
                    (t if h == H - 1 else t.e).matmul(
                        po, q2[:, h * 64:(h + 1) * 64],
                        S[:, h * 256:(h + 1) * 256],
                        start=(h % 2 == 0), stop=(h % 2 == 1))
            self.TE(jt)
            corr_mm[c] = self.pe.n
            tn = self.pe.n

            def jv2(v, c=c, tb=tb, prow=prow, tn=tn, pa=pa, pb=pb):
                v.wait(self.pe, tn)
                v.tensor_tensor(obuf[prow:prow + 64, tb * 2048:tb * 2048 + 1024],
                                obuf[prow:prow + 64, tb * 2048:tb * 2048 + 1024],
                                pa[prow:prow + 64, 0:1024], ALU.add)
                v.tensor_tensor(obuf[prow:prow + 64, tb * 2048 + 1024:(tb + 1) * 2048],
                                obuf[prow:prow + 64, tb * 2048 + 1024:(tb + 1) * 2048],
                                pb[prow:prow + 64, 0:1024], ALU.add)
            self.VE(jv2)
            corr_ev[c] = self.ve.n
        self.corr_ve = self.ve.n

    def p3d_epilogue(self):
        ntok, ntb = self.ntok, self.ntb
        obuf = self.RE
        gate = self.RB
        sq = self.RFf[:, 0:2048]
        silu = self.RF[:, 4096:6144]
        ogs = self.RF[:, 6144:8192]
        rr4 = self.RFf[:, 4152:4160]
        epi_ev = {}
        for tb in range(ntb):
            def jse(s, tb=tb):
                if tb == 0:
                    s.wait(self.ve, self.corr_ve)
                else:
                    s.wait(self.ve, epi_ev[tb - 1])
                s.activation(sq, obuf[:, tb * 2048:(tb + 1) * 2048], AF.Square)
                s.wait(self.se)
                s.activation(silu, gate[:, tb * 2048:(tb + 1) * 2048], AF.Silu)
            self.SE(jse)
            sen = self.se.n

            def jv(v, sen=sen):
                v.wait(self.se, sen)
                v.tensor_reduce(rr4, sq.rearrange("p (h v) -> p h v", h=H),
                                AX.X, ALU.add)
                v.wait(self.ve)
                v.tensor_scalar(rr4, rr4, 1.0 / HV, EPS_RMS, ALU.mult, op1=ALU.add)
                v.wait(self.ve)
                v.reciprocal(rr4, rr4)
            self.VE(jv)
            vn = self.ve.n

            def jse2(s, vn=vn):
                s.wait(self.ve, vn)
                s.activation(rr4, rr4, AF.Sqrt)
            self.SE(jse2)
            sen2 = self.se.n

            def jv2(v, tb=tb, sen2=sen2):
                v.wait(self.se, sen2)
                for h in range(H):
                    v.tensor_scalar(ogs[:, h * 256:(h + 1) * 256],
                                    obuf[:, tb * 2048 + h * 256:tb * 2048 + (h + 1) * 256],
                                    rr4[:, h:h + 1], None, ALU.mult)
                v.wait(self.ve)
                v.tensor_tensor(ogs, ogs, silu, ALU.mult)
            self.VE(jv2)
            vn2 = self.ve.n

            def jt(t, tb=tb, vn2=vn2):
                t.wait(self.ve, vn2)
                if tb >= 1:
                    t.wait(self.ve, epi_ev[tb - 1])
                for c in range(NKC):
                    dst = (self.pCb if c < 8 else self.pDb)[:, (c % 8) * 128:(c % 8 + 1) * 128]
                    (t if c == NKC - 1 else t.e).transpose(
                        dst, ogs[:, c * 128:(c + 1) * 128], self.cidb[:])
            self.TE(jt)
            tn = self.pe.n

            def jv3(v, tb=tb, tn=tn):
                v.wait(self.pe, tn)
                for c in range(NKC):
                    src = (self.pCb if c < 8 else self.pDb)[:, (c % 8) * 128:(c % 8 + 1) * 128]
                    v.tensor_copy(self.RC[:, c * ntok + tb * 128:c * ntok + (tb + 1) * 128], src)
            self.VE(jv3)
            epi_ev[tb] = self.ve.n
        self.epi_ev = epi_ev
        self.epi_ve = self.ve.n

    def p4_wo(self):
        ntok, ntb = self.ntok, self.ntb
        og = self.RC
        acc = self.RE
        xnt = self.RF[:, 8448:10496]
        h2t = self.RF[:, 2048:4096]
        sqsc = self.RFf[:, 2048:2560]
        sm = self.smf
        banks4 = [self.pA[:, 0:512], self.pA[:, 512:1024],
                  self.pB[:, 0:512], self.pB[:, 512:1024]]
        wo_ev = {}
        for ds in range(4):
            li4 = [self.lq_push(
                self.dwo.ap()[ds, g4],
                scale_cols=[(self.cgnw[:, ((g4 * 4 + f) % 2):((g4 * 4 + f) % 2) + 1],
                             f * 512, (f + 1) * 512) for f in range(4)])
                for g4 in range(NKC // 4)]
            rdy4 = [self.w8(li) for li in li4]
            for tb in range(ntb):
                def jsy(s, ds=ds, tb=tb):
                    s.wait(self.ve, self.epi_ev[tb])
                    if (ds - 1, tb) in wo_ev:
                        s.wait(self.ve, wo_ev[(ds - 1, tb)])
                    if tb > 0:
                        s.wait(self.ve, wo_ev[(ds, tb - 1)])
                    s.wait(self.sp, self.spill_gd[tb])
                    s.wait(self.sy)
                    s.dma_start(out=xnt[:, ds * 512:(ds + 1) * 512],
                                in_=self.dspill.ap()[tb * 128:(tb + 1) * 128,
                                                     ds * 512:(ds + 1) * 512])
                self.SY(jsy)
                xld = self.sy.n
                bank = banks4[tb % 4]

                def jt(t, ds=ds, tb=tb, bank=bank, rdy4=rdy4):
                    t.wait(self.ve, self.epi_ev[tb])
                    if tb >= 4:
                        t.wait(self.ve, wo_ev[(ds, tb - 4)])
                    elif (ds - 1, tb + 4) in wo_ev:
                        t.wait(self.ve, wo_ev[(ds - 1, tb + 4)])
                    for g4 in range(NKC // 4):
                        t.wait(rdy4[g4][0], rdy4[g4][1])
                        w = rdy4[g4][2]
                        for f in range(4):
                            c = g4 * 4 + f
                            sign = (c == NKC - 1) or (tb == ntb - 1 and f == 3)
                            (t if sign else t.e).matmul(
                                bank,
                                og[:, c * ntok + tb * 128:c * ntok + (tb + 1) * 128],
                                w[:, f * 512:(f + 1) * 512],
                                start=(c == 0), stop=(c == NKC - 1))
                        if tb == ntb - 1:
                            self._womk[g4] = self.pe.n
                self.TE(jt)
                if tb == 0:
                    self._womk = {}
                mmn = self.pe.n

                def jv(v, ds=ds, tb=tb, bank=bank, mmn=mmn, xld=xld):
                    v.wait(self.pe, mmn)
                    v.e.wait_ge(self.sy.sem, xld)
                    v.tensor_tensor(acc[:, tb * 2048 + ds * 512:tb * 2048 + (ds + 1) * 512],
                                    bank, xnt[:, ds * 512:(ds + 1) * 512], ALU.add)
                self.VE(jv)
                wo_ev[(ds, tb)] = self.ve.n
            for g4, li in enumerate(li4):
                self.free_slot(li, (self.pe, self._womk[g4]))
                self.lq_need(li + len(self.ring))
        acc_all = self.ve.n

        for tb in range(ntb):
            def jse(s, tb=tb):
                s.wait(self.ve, acc_all if tb == 0 else wo_ev[("h2", tb - 1)])
                for ds in range(4):
                    if ds:
                        s.wait(self.se)
                    s.activation(sqsc, acc[:, tb * 2048 + ds * 512:tb * 2048 + (ds + 1) * 512],
                                 AF.Square, accum_out=sm[:, 20 + ds:21 + ds])
            self.SE(jse)
            sen = self.se.n

            def jv2(v, sen=sen):
                v.wait(self.se, sen)
                v.tensor_reduce(sm[:, 24:25], sm[:, 20:24], AX.X, ALU.add)
                v.wait(self.ve)
                v.tensor_scalar(sm[:, 24:25], sm[:, 24:25], 1.0 / D, EPS_RMS,
                                ALU.mult, op1=ALU.add)
                v.wait(self.ve)
                v.reciprocal(sm[:, 25:26], sm[:, 24:25])
            self.VE(jv2)
            vn = self.ve.n

            def jse2(s, vn=vn):
                s.wait(self.ve, vn)
                s.activation(sm[:, 26:27], sm[:, 25:26], AF.Sqrt)
            self.SE(jse2)
            sen2 = self.se.n

            def jv3(v, tb=tb, sen2=sen2):
                s = sen2
                v.wait(self.se, sen2)
                if tb >= 1:
                    v.wait(self.pe, wo_ev[("h2pe", tb - 1)])
                v.tensor_scalar(h2t, acc[:, tb * 2048:(tb + 1) * 2048],
                                sm[:, 26:27], None, ALU.mult)
            self.VE(jv3)
            vn3 = self.ve.n

            def jt2(t, vn3=vn3, tb=tb):
                t.wait(self.ve, vn3)
                if tb >= 1:
                    t.wait(self.ve, wo_ev[("h2", tb - 1)])
                for c in range(NKC):
                    (t if c == NKC - 1 else t.e).transpose(
                        self.pDb[:, c * 128:(c + 1) * 128],
                        h2t[:, c * 128:(c + 1) * 128], self.cidb[:])
            self.TE(jt2)
            wo_ev[("h2pe", tb)] = self.pe.n
            tn = self.pe.n

            def jv4(v, tb=tb, tn=tn):
                v.wait(self.pe, tn)
                for c in range(NKC):
                    v.tensor_scalar(
                        self.RD[:, c * ntok + tb * 128:c * ntok + (tb + 1) * 128],
                        self.pDb[:, c * 128:(c + 1) * 128],
                        self.cfnw[:, c:c + 1], None, ALU.mult)
            self.VE(jv4)
            wo_ev[("h2", tb)] = self.ve.n
        self.p4_ve = self.ve.n

    def p4b_ffn(self):
        ntok, ntb, NFC = self.ntok, self.ntb, self.NFC
        h2 = self.RD
        acc = self.RE
        sg = self.RF[:, 4096:6144]
        banks = [self.pA[:, 0:512], self.pA[:, 512:1024],
                 self.pB[:, 0:512], self.pB[:, 512:1024]]
        nth = max(1, ntok // 512)
        tw = min(512, ntok)
        upw = min(1024, ntok)
        npass = max(1, ntb // 4)
        ta = min(16, ((NFC + 11) // 12) * 4)
        tb_ = min(32, 2 * ta)
        thirds = [t for t in [(0, ta), (ta, tb_), (tb_, NFC)] if t[1] > t[0]]
        prev = {"dn_ev": self.p4_ve}

        def mid_of(t3):
            return self.RB if t3 % 2 == 0 else self.RC

        def up_unit(t3, jloc, lis):
            mid = mid_of(t3)
            bga, bgb = self.pC, self.pD
            rg_ = self.w8(lis[jloc][0])
            ru_ = self.w8(lis[jloc][1])

            def jt(t, t3=t3, jloc=jloc, rg_=rg_, ru_=ru_, bga=bga, bgb=bgb):
                if jloc == 0 and t3 == 0:
                    t.wait(self.ve, self.p4_ve)
                    t.wait(self.ve, prev["dn_ev"])
                if jloc == 0 and t3 >= 2:
                    t.wait(self.ve, prev[("dn_done", t3 - 2)])
                if "se" in prev:
                    t.wait(self.se, prev["se"])
                t.wait(rg_[0], rg_[1])
                w = rg_[2]
                for th in range(nth):
                    for c in range(NKC):
                        (t if c == NKC - 1 and th == nth - 1 else t.e).matmul(
                            bga[:, th * 512:th * 512 + tw],
                            w[:, c * 128:(c + 1) * 128],
                            h2[:, c * ntok + th * 512:c * ntok + th * 512 + tw],
                            start=(c == 0), stop=(c == NKC - 1))
                self._gmark = self.pe.n
                if "a" in prev:
                    t.wait(self.ve, prev["a"])
                t.wait(ru_[0], ru_[1])
                w = ru_[2]
                for th in range(nth):
                    for c in range(NKC):
                        (t if c == NKC - 1 and th == nth - 1 else t.e).matmul(
                            bgb[:, th * 512:th * 512 + tw],
                            w[:, c * 128:(c + 1) * 128],
                            h2[:, c * ntok + th * 512:c * ntok + th * 512 + tw],
                            start=(c == 0), stop=(c == NKC - 1))
            self.TE(jt)
            mmn = self.pe.n
            self.free_slot(lis[jloc][0], (self.pe, self._gmark))
            self.lq_need(lis[jloc][0] + len(self.ring))
            self.done(lis[jloc][1])

            def jse(s, mmn=mmn, bga=bga):
                s.wait(self.pe, mmn)
                s.activation(sg[:, 0:upw], bga[:, 0:upw], AF.Silu)
            self.SE(jse)
            sen = self.se.n
            prev["se"] = sen

            def jv(v, t3=t3, jloc=jloc, sen=sen, mid=mid, bgb=bgb):
                v.wait(self.se, sen)
                v.tensor_tensor(mid[:, jloc * ntok:jloc * ntok + upw],
                                sg[:, 0:upw], bgb[:, 0:upw], ALU.mult)
            self.VE(jv)
            prev["a"] = self.ve.n

        def dn_unit(t3, ds, ps, lis2, mid_rdy, nfc_loc):
            mid = mid_of(t3)
            tbs = list(range(ps * 4, min(ps * 4 + 4, ntb)))

            def jt(t, t3=t3, ds=ds, lis2=lis2, mid_rdy=mid_rdy,
                   nfc_loc=nfc_loc, tbs=tbs):
                t.wait(self.ve, mid_rdy)
                t.wait(self.ve, prev["dn_ev"])
                for (i0, gn, li) in lis2:
                    t.wait(*self.lq_ready[li])
                    w = self.ring[li % len(self.ring)]
                    for f in range(gn):
                        i = i0 + f
                        for tb in tbs:
                            ((t if tb == tbs[-1] and f == gn - 1 else t.e)
                             .matmul)(banks[tb % 4],
                                      mid[:, i * ntok + tb * 128:i * ntok + (tb + 1) * 128],
                                      w[:, f * 512:(f + 1) * 512],
                                      start=(i == 0), stop=(i == nfc_loc - 1))
            self.TE(jt)
            mmn = self.pe.n

            def jv(v, ds=ds, mmn=mmn, tbs=tbs):
                v.wait(self.pe, mmn)
                for tb in tbs:
                    sl = acc[:, tb * 2048 + ds * 512:tb * 2048 + (ds + 1) * 512]
                    v.tensor_tensor(sl, sl, banks[tb % 4], ALU.add)
            self.VE(jv)
            prev["dn_ev"] = self.ve.n

        def push_dn(t3, ds):
            f0, f1 = thirds[t3]
            return [(g * 4 - f0, min(4, f1 - g * 4),
                     self.lq_push(self.dwdn.ap()[ds, g]))
                    for g in range(f0 // 4, (f1 + 3) // 4)]

        dn_pend = None
        for t3, (f0, f1) in enumerate(thirds):
            nfc_loc = f1 - f0
            lis = []
            lis2_cache = {}
            dn_units = []
            if dn_pend is not None:
                pt3, pmid, pnfc = dn_pend
                dn_units = [(pt3, ds, ps, pmid, pnfc)
                            for ds in range(4) for ps in range(npass)]
            nup = nfc_loc
            di = 0
            for ui in range(nup):
                lis.append((self.lq_push(self.dwgate.ap()[f0 + ui]),
                            self.lq_push(self.dwup.ap()[f0 + ui])))
                up_unit(t3, ui, lis)
                while di < len(dn_units) and (di + 1) * nup <= (ui + 1) * len(dn_units):
                    pt3, ds, ps, pmid, pnfc = dn_units[di]
                    lis2 = push_dn(pt3, ds)
                    for (i0, gn, li) in lis2:
                        self.w8(li)
                    dn_unit(pt3, ds, ps, lis2, pmid, pnfc)
                    for (i0, gn, li) in lis2:
                        self.done(li)
                    di += 1
            while di < len(dn_units):
                pt3, ds, ps, pmid, pnfc = dn_units[di]
                lis2 = push_dn(pt3, ds)
                for (i0, gn, li) in lis2:
                    self.w8(li)
                dn_unit(pt3, ds, ps, lis2, pmid, pnfc)
                for (i0, gn, li) in lis2:
                    self.done(li)
                di += 1
            if dn_pend is not None:
                prev[("dn_done", dn_pend[0])] = self.ve.n
            dn_pend = (t3, self.ve.n, nfc_loc)

        pt3, pmid, pnfc = dn_pend
        lis2_cache = {}
        for ds in range(4):
            for ps in range(npass):
                if ps == 0:
                    lis2_cache[ds] = push_dn(pt3, ds)
                    for (i0, gn, li) in lis2_cache[ds]:
                        self.w8(li)
                dn_unit(pt3, ds, ps, lis2_cache[ds], pmid, pnfc)
                if ps == npass - 1:
                    for (i0, gn, li) in lis2_cache[ds]:
                        self.done(li)
        self.p4b_ve = self.ve.n

    def p5_moe(self):
        ntok, ntb, E, FSH = self.ntok, self.ntb, self.E, self.FSH
        acc = self.RE
        sg = self.RF[:, 4096:6144]
        banks = [self.pA[:, 0:512], self.pA[:, 512:1024],
                 self.pB[:, 0:512], self.pB[:, 512:1024],
                 self.pC[:, 0:512], self.pC[:, 512:1024],
                 self.pD[:, 0:512], self.pD[:, 512:1024]]
        nth = max(1, ntok // 512)
        tw = min(512, ntok)
        npass = max(1, ntb // 4)
        prev = {"dn_ev": self.p4b_ve, "up_done": {}}

        def midslot(ei):
            return self.RD[:, (ei % 2) * FSH * ntok:((ei % 2) + 1) * FSH * ntok]

        def up_srcs(ei):
            if ei == 0:
                return [(self.dsgw.ap()[j], self.dsuw.ap()[j]) for j in range(FSH)]
            e = ei - 1
            return [(self.degu.ap()[e, j], self.degu.ap()[e, FSH + j])
                    for j in range(FSH)]

        def dn_src(ei, ds, g):
            if ei == 0:
                return self.dsdw.ap()[ds, g]
            return self.ded.ap()[ei - 1, ds, g]

        def comb_col(ei, tb):
            if ei == 0:
                return None
            e = ei - 1
            return self.comb[:, tb * E + e:tb * E + e + 1]

        upw = min(1024, ntok)

        def up_unit(ei, j, lis):
            mid = midslot(ei)
            bga, bgb = self.pC, self.pD
            rg_ = self.w8(lis[j][0])
            ru_ = self.w8(lis[j][1])

            def jt(t, ei=ei, j=j, rg_=rg_, ru_=ru_, bga=bga, bgb=bgb):
                if j == 0 and ei == 0:
                    t.wait(self.ve, prev["dn_ev"])
                if j == 0 and ei >= 2:
                    t.wait(self.ve, prev[("dn_done", ei - 2)])
                if "se" in prev:
                    t.wait(self.se, prev["se"])
                t.wait(rg_[0], rg_[1])
                w = rg_[2]
                for th in range(nth):
                    for c in range(NKC):
                        (t if c == NKC - 1 and th == nth - 1 else t.e).matmul(
                            bga[:, th * 512:th * 512 + tw],
                            w[:, c * 128:(c + 1) * 128],
                            self.xn_rhs(c, th), start=(c == 0), stop=(c == NKC - 1))
                self._gmark = self.pe.n
                if "a" in prev:
                    t.wait(self.ve, prev["a"])
                t.wait(ru_[0], ru_[1])
                w = ru_[2]
                for th in range(nth):
                    for c in range(NKC):
                        (t if c == NKC - 1 and th == nth - 1 else t.e).matmul(
                            bgb[:, th * 512:th * 512 + tw],
                            w[:, c * 128:(c + 1) * 128],
                            self.xn_rhs(c, th), start=(c == 0), stop=(c == NKC - 1))
            self.TE(jt)
            mmn = self.pe.n
            self.free_slot(lis[j][0], (self.pe, self._gmark))
            self.lq_need(lis[j][0] + len(self.ring))
            self.done(lis[j][1])

            def jse(s, mmn=mmn, bga=bga):
                s.wait(self.pe, mmn)
                s.activation(sg[:, 0:upw], bga[:, 0:upw], AF.Silu)
            self.SE(jse)
            sen = self.se.n
            prev["se"] = sen

            def jv(v, j=j, sen=sen, mid=mid, bgb=bgb):
                v.wait(self.se, sen)
                v.tensor_tensor(mid[:, j * ntok:j * ntok + upw],
                                sg[:, 0:upw], bgb[:, 0:upw], ALU.mult)
            self.VE(jv)
            prev["a"] = self.ve.n

        def dn_unit(ei, ds, ps, lis2, mid_rdy):
            mid = midslot(ei)
            tbs = list(range(ps * 4, min(ps * 4 + 4, ntb)))

            def jt(t, ei=ei, ds=ds, ps=ps, lis2=lis2, mid_rdy=mid_rdy, tbs=tbs):
                t.wait(self.ve, mid_rdy)
                t.wait(self.ve, prev["dn_ev"])
                for (i0, gn, li) in lis2:
                    t.wait(*self.lq_ready[li])
                    w = self.ring[li % len(self.ring)]
                    for f in range(gn):
                        i = i0 + f
                        for tb in tbs:
                            ((t if tb == tbs[-1] and f == gn - 1 else t.e)
                             .matmul)(banks[tb % 4],
                                      mid[:, i * ntok + tb * 128:i * ntok + (tb + 1) * 128],
                                      w[:, f * 512:(f + 1) * 512],
                                      start=(i == 0), stop=(i == FSH - 1))
            self.TE(jt)
            mmn = self.pe.n

            def jv(v, ei=ei, ds=ds, mmn=mmn, tbs=tbs):
                v.wait(self.pe, mmn)
                for tb in tbs:
                    col = comb_col(ei, tb)
                    sl = acc[:, tb * 2048 + ds * 512:tb * 2048 + (ds + 1) * 512]
                    if col is None:
                        v.tensor_tensor(sl, sl, banks[tb % 4], ALU.add)
                    else:
                        v.scalar_tensor_tensor(sl, banks[tb % 4], col, sl,
                                               ALU.mult, ALU.add)
            self.VE(jv)
            prev["dn_ev"] = self.ve.n

        def push_dn_group(ei, ds):
            return [(g * 4, 4, self.lq_push(dn_src(ei, ds, g)))
                    for g in range(FSH // 4)]

        def run_dn(pei, ds, ps, lis2_cache, pmid):
            if ps == 0:
                lis2_cache[ds] = push_dn_group(pei, ds)
                for (i0, gn, li) in lis2_cache[ds]:
                    self.w8(li)
            dn_unit(pei, ds, ps, lis2_cache[ds], pmid)
            if ps == npass - 1:
                for (i0, gn, li) in lis2_cache[ds]:
                    self.done(li)

        dn_pend = None
        for ei in range(E + 1):
            srcs = up_srcs(ei)
            lis = []
            upj = list(range(FSH))
            dn_units = []
            lis2_cache = {}
            if dn_pend is not None:
                pei, pmid = dn_pend
                dn_units = [(pei, ds, ps, pmid)
                            for ds in range(4) for ps in range(npass)]
            di = 0
            for ui, j in enumerate(upj):
                lis.append((self.lq_push(srcs[j][0]), self.lq_push(srcs[j][1])))
                up_unit(ei, j, lis)
                while di < len(dn_units) and (di + 1) * len(upj) <= (ui + 1) * len(dn_units):
                    pei, ds, ps, pmid = dn_units[di]
                    run_dn(pei, ds, ps, lis2_cache, pmid)
                    di += 1
            while di < len(dn_units):
                pei, ds, ps, pmid = dn_units[di]
                run_dn(pei, ds, ps, lis2_cache, pmid)
                di += 1
            if dn_pend is not None:
                prev[("dn_done", dn_pend[0])] = self.ve.n
            mid_rdy = self.ve.n
            dn_pend = (ei, mid_rdy)

        pei, pmid = dn_pend
        lis2_cache = {}
        for ds in range(4):
            for ps in range(npass):
                run_dn(pei, ds, ps, lis2_cache, pmid)
        self.p5_ve = self.ve.n

    def p6_out(self):
        ntb = self.ntb
        acc = self.RE
        xrd = self.RFf[:, 2048:4096]
        ob = self.RFf[:, 0:2048]
        out_ev = {}
        for tb in range(ntb):
            def jsy(s, tb=tb):
                if tb == 0:
                    s.wait(self.ve, self.p5_ve)
                else:
                    s.wait(self.ve, out_ev[tb - 1])
                s.dma_start(out=xrd, in_=self.dx.ap()[tb * 128:(tb + 1) * 128, :])
            self.SY(jsy)
            xld = self.sy.n

            def jv(v, tb=tb, xld=xld):
                v.e.wait_ge(self.sy.sem, xld)
                if tb == 0:
                    v.wait(self.ve, self.p5_ve)
                else:
                    v.e.wait_ge(self.so.sem, 16 * tb)
                v.tensor_tensor(ob, acc[:, tb * 2048:(tb + 1) * 2048], xrd, ALU.add)
            self.VE(jv)
            out_ev[tb] = self.ve.n
            vn = self.ve.n

            def jst(s, tb=tb, vn=vn):
                s.e.wait_ge(self.ve.sem, vn)
                self.so.sig(s.e.dma_start(
                    out=self.dout.ap()[tb * 128:(tb + 1) * 128, :], in_=ob))
            self.SY(jst)

        def jfin(s):
            s.e.wait_ge(self.so.sem, 16 * ntb)
        self.SY(jfin)


def build_program(cfg):
    return MK(cfg).build()


def _padc(w, m=512):
    F = w.shape[1]
    pad = (-F) % m
    if pad == 0:
        return w
    return np.concatenate([w, np.zeros((w.shape[0], pad), w.dtype)], axis=1)


def _padr(w, m=512):
    F = w.shape[0]
    pad = (-F) % m
    if pad == 0:
        return w
    return np.concatenate([w, np.zeros((pad, w.shape[1]), w.dtype)], axis=0)


def _tile_up(w):
    Dd, F = w.shape
    return np.ascontiguousarray(
        w.reshape(Dd // 128, 128, F // 128, 128).transpose(2, 1, 0, 3)
        .reshape(F // 128, 128, Dd))


def _tile_down(w):
    F, Dout = w.shape
    assert F % 512 == 0
    return np.ascontiguousarray(
        w.reshape(F // 512, 4, 128, 4, Dout // 4).transpose(3, 0, 2, 1, 4)
        .reshape(4, F // 512, 128, 4 * (Dout // 4)))


def kernel(**inputs):
    cfg = dict(ntb=8, E=16, nfc=44, fsh=8)
    ntok = cfg["ntb"] * 128

    x = np.asarray(inputs["L0_attn_x"], dtype=np.float32)
    B, T, _ = x.shape
    nseg = NCORES // B

    f32 = lambda a: np.asarray(a, dtype=np.float32)
    shared = dict(
        wq=_tile_up(f32(inputs["L0_attn_wq"])),
        wk=_tile_up(f32(inputs["L0_attn_wk"])),
        wv=_tile_up(f32(inputs["L0_attn_wv"])),
        wg=_tile_up(f32(inputs["L0_attn_wg"])),
        gk1=np.ascontiguousarray(
            f32(inputs["L0_attn_gk_w1"]).reshape(NKC, 128, 16)
            .transpose(1, 0, 2).reshape(128, NKC * 16)),
        gk2=np.ascontiguousarray(f32(inputs["L0_attn_gk_w2"])),
        gkb=np.ascontiguousarray(f32(inputs["L0_attn_gk_b"]).reshape(8, 128).T),
        wo=_tile_down(f32(inputs["L0_attn_wo"])),
        n1w=np.ascontiguousarray(f32(inputs["L0_attn_norm1_w"]).reshape(NKC, 128).T),
        fnw=np.ascontiguousarray(f32(inputs["L0_attn_ffn_norm_w"]).reshape(NKC, 128).T),
        gnw=np.ascontiguousarray(f32(inputs["L0_attn_gnorm_w"]).reshape(2, 128).T),
        wgate=_tile_up(_padc(f32(inputs["L0_attn_w_gate"]))),
        wup=_tile_up(_padc(f32(inputs["L0_attn_w_up"]))),
        wdn=_tile_down(_padr(f32(inputs["L0_attn_w_down"]))),
        rtw=np.ascontiguousarray(
            f32(inputs["L0_ffn_gate_w"]).reshape(NKC, 128, cfg["E"])
            .transpose(1, 0, 2).reshape(128, NKC * cfg["E"])),
        ebias=np.ascontiguousarray(f32(inputs["L0_ffn_expert_bias"]).reshape(1, -1)),
        sgw=_tile_up(f32(inputs["L0_ffn_shared_gate_w"])),
        suw=_tile_up(f32(inputs["L0_ffn_shared_up_w"])),
        sdw=_tile_down(f32(inputs["L0_ffn_shared_down_w"])),
        egu=np.stack([_tile_up(f32(inputs["L0_ffn_experts_gate_up"])[e])
                      for e in range(cfg["E"])]),
        ed=np.stack([_tile_down(f32(inputs["L0_ffn_experts_down"])[e])
                     for e in range(cfg["E"])]),
    )
    shared = {k: np.ascontiguousarray(v, dtype=np.float32) for k, v in shared.items()}

    nc = build_program(cfg)

    in_maps = []
    for core in range(NCORES):
        b, s = core // nseg, core % nseg
        sel = np.zeros((128, NCORES), dtype=np.float32)
        for j in range(NCORES):
            if j // nseg == b and j % nseg < s:
                sel[:, j] = 1.0
        m = dict(shared)
        m["x"] = np.ascontiguousarray(x[b, s * ntok:(s + 1) * ntok, :])
        m["selmask"] = sel
        in_maps.append(m)

    res = run_bass_kernel_spmd(nc, in_maps, core_ids=list(range(NCORES)))
    out = np.empty((B, T, D), dtype=np.float32)
    for core in range(NCORES):
        b, s = core // nseg, core % nseg
        out[b, s * ntok:(s + 1) * ntok, :] = res.results[core]["out"]
    return out


# version 16 (final)
# speedup vs baseline: 1.0156x; 1.0156x over previous
import numpy as np
from contextlib import ExitStack
import concourse.bass as bass
import concourse.mybir as mybir
import concourse.bass_utils as _bu
from concourse.bass_utils import run_bass_kernel_spmd

if not getattr(_bu, "_mk_walrus_patch", False):
    _orig_gwa = _bu.get_walrus_args

    def _gwa(*a, **kw):
        return _orig_gwa(*a, **kw) + ["--enable-dynamic-act-table"]
    _bu.get_walrus_args = _gwa
    _bu._mk_walrus_patch = True

F32 = mybir.dt.float32
BF16 = mybir.dt.bfloat16
AF = mybir.ActivationFunctionType
ALU = mybir.AluOpType
AX = mybir.AxisListType

NCORES = 8
D = 2048
NKC = 16
H = 8
HK = 128
HV = 256
KD = 1024
VD = 2048
CH = 64
EPS_LN = 1e-5
EPS_RMS = 1e-6
S16 = 1.0 / 16.0


class Ctr:
    def __init__(self, sem, step=1):
        self.sem, self.step, self.n = sem, step, 0

    def sig(self, instr, k=1):
        instr.then_inc(self.sem, self.step * k)
        self.n += self.step * k
        return self.n


class EW:
    def __init__(self, e, ctr):
        self.e, self.c = e, ctr

    def __getattr__(self, name):
        f = getattr(self.e, name)

        def g(*a, **k):
            r = f(*a, **k)
            if r is not None and hasattr(r, "then_inc"):
                self.c.sig(r)
            return r
        return g

    def wait(self, ctr, val=None):
        self.e.wait_ge(ctr.sem, ctr.n if val is None else val)


class MK:
    def __init__(self, cfg):
        self.cfg = cfg
        self.ntb = cfg["ntb"]
        self.ntok = 128 * self.ntb
        self.nch = self.ntok // CH
        self.E = cfg["E"]
        self.NFC = cfg["nfc"]
        self.FSH = cfg["fsh"]
        self.nc = bass.Bass("TRN2", target_bir_lowering=False)

    def TE(self, f):
        self.block.tensor(lambda e: f(EW(e, self.pe)))

    def VE(self, f):
        self.block.vector(lambda e: f(EW(e, self.ve)))

    def SE(self, f):
        self.block.scalar(lambda e: f(EW(e, self.se)))

    def SY(self, f):
        self.block.sync(lambda e: f(EW(e, self.sy)))

    def GP(self, f):
        self.block.gpsimd(lambda e: f(EW(e, self.gd)))

    def declare(self):
        nc = self.nc
        ntok, E, NFC, FSH = self.ntok, self.E, self.NFC, self.FSH
        inp = lambda n, s: nc.dram_tensor(n, s, F32, kind="ExternalInput")
        self.dx = inp("x", [ntok, D])
        self.dsel = inp("selmask", [128, NCORES])
        self.dwq = inp("wq", [KD // 128, 128, D])
        self.dwk = inp("wk", [KD // 128, 128, D])
        self.dwv = inp("wv", [VD // 128, 128, D])
        self.dwg = inp("wg", [VD // 128, 128, D])
        self.dgk1 = inp("gk1", [128, NKC * 16])
        self.dgk2 = inp("gk2", [16, KD])
        self.dgkb = inp("gkb", [128, KD // 128])
        self.dwo = inp("wo", [4, NKC // 4, 128, 2048])
        self.dn1w = inp("n1w", [128, NKC])
        self.dfnw = inp("fnw", [128, NKC])
        self.dgnw = inp("gnw", [128, 2])
        self.dwgate = inp("wgate", [NFC, 128, D])
        self.dwup = inp("wup", [NFC, 128, D])
        self.dwdn = inp("wdn", [4, NFC // 4, 128, 2048])
        self.drtw = inp("rtw", [128, NKC * E])
        self.debias = inp("ebias", [1, E])
        self.dsgw = inp("sgw", [FSH, 128, D])
        self.dsuw = inp("suw", [FSH, 128, D])
        self.dsdw = inp("sdw", [4, FSH // 4, 128, 2048])
        self.degu = inp("egu", [E, NKC, 128, D])
        self.ded = inp("ed", [E, 4, FSH // 4, 128, 2048])
        self.dout = nc.dram_tensor("out", [ntok, D], F32, kind="ExternalOutput")
        self.dspill = nc.dram_tensor("xnspill", [ntok, D], BF16, kind="Internal")
        self.CCR = H * HK + 8
        self.dccin = nc.dram_tensor("ccin", [self.CCR, HV], F32, kind="Internal")
        self.dccout = nc.dram_tensor("ccout", [NCORES * self.CCR, HV], F32,
                                     kind="Internal", addr_space="Shared")
        import ml_dtypes
        self.didf = nc.inline_tensor(np.eye(128, dtype=np.float32), "c_idf")
        self.didb = nc.inline_tensor(np.eye(128).astype(ml_dtypes.bfloat16), "c_idb")
        import ml_dtypes as _mld
        m = (np.arange(CH)[None, :] >= np.arange(CH)[:, None]).astype(np.float32)
        self.dmask = nc.inline_tensor(
            np.tile(np.concatenate([m, m], axis=0), (1, H)).astype(_mld.bfloat16),
            "c_mask")
        self.dones = nc.inline_tensor(np.ones((1, 128), dtype=np.float32), "c_ones")

    def alloc(self, st):
        nc, ntok = self.nc, self.ntok
        E = self.E
        sb = lambda n, s, d: st.enter_context(nc.sbuf_tensor(n, s, d))
        ps = lambda n, s: st.enter_context(nc.psum_tensor(n, s, F32))
        sem = lambda n: st.enter_context(nc.semaphore(n))
        self.RA = sb("RA", [128, NKC * ntok], BF16)
        self.RB = sb("RB", [128, NKC * ntok], BF16)
        self.RC = sb("RC", [128, NKC * ntok], BF16)
        self.RD = sb("RD", [128, NKC * ntok], BF16)
        self.RE = sb("RE", [128, max(NKC * ntok, 13312)], BF16)
        self.RF = sb("RF", [128, 10496], BF16)
        self.ring = [sb("rg%d" % i, [128, D], BF16) for i in range(5)]
        self.cid = sb("cid", [128, 128], F32)
        self.cidb = sb("cidb", [128, 128], BF16)
        self.cmask = sb("cmask", [128, H * CH], BF16)
        self.cones = sb("cones", [1, 128], F32)
        self.cn1w = sb("cn1w", [128, NKC], F32)
        self.cn1wq = sb("cn1wq", [128, NKC], F32)
        self.cfnw = sb("cfnw", [128, NKC], F32)
        self.cgnw = sb("cgnw", [128, 2], F32)
        self.cgkb = sb("cgkb", [128, KD // 128], F32)
        self.csel = sb("csel", [128, NCORES], F32)
        self.cebias = sb("cebias", [128, E], F32)
        self.cgk2 = sb("cgk2", [16, KD], BF16)
        self.rr2tm = sb("rr2tm", [128, self.ntb], F32)
        self.comb = sb("comb", [128, self.ntb * E], F32)
        self.vts = sb("vts", [128, 512], BF16)
        self.smf = sb("smf", [128, 64], F32)
        self.pA = ps("pA", [128, 1024])
        self.pB = ps("pB", [128, 1024])
        self.pC = ps("pC", [128, 1024])
        self.pD = ps("pD", [128, 1024])
        self.pe = Ctr(sem("s_pe"))
        self.ve = Ctr(sem("s_ve"))
        self.se = Ctr(sem("s_se"))
        self.sy = Ctr(sem("s_sy"), 16)
        self.gd = Ctr(sem("s_gd"), 16)
        self.cd = Ctr(sem("s_cd"), 16)
        self.cc = Ctr(sem("s_cc"))
        self.so = Ctr(sem("s_out"), 16)
        self.gsl = [Ctr(sem("s_gsl%d" % i), 16) for i in range(len(self.ring))]
        self.sp = Ctr(sem("s_sp"), 16)
        self.RBf = self.RB.bitcast(F32)
        RFf_ = self.RF.bitcast(F32)
        self.cgw = RFf_[:, 2048:2048 + NKC * E]
        self.glrt = RFf_[:, 2048 + NKC * E:2048 + NKC * E + self.ntb * 16]
        self.cgk1 = self.RE.bitcast(BF16)[:, 12288:12288 + NKC * 16] if False \
            else self.RE[:, 12288:12288 + NKC * 16]
        self.RDf = self.RD.bitcast(F32)
        self.cumbl = self.RDf[:, 4 * ntok:4 * ntok + H * self.nch]
        self.RFf = self.RF.bitcast(F32)
        self.REf = self.RE.bitcast(F32)
        self.pAb = self.pA.bitcast(BF16)
        self.pCb = self.pC.bitcast(BF16)
        self.pDb = self.pD.bitcast(BF16)
        self.lq = []
        self.lq_emitted = 0
        self.lq_ready = []
        self.ring_free = [None] * len(self.ring)
        self.ring_free_li = list(range(-len(self.ring), 0))

    def lq_push(self, src_ap, scale_cols=None, width=D):
        self.lq.append((src_ap, scale_cols, width, self.ve.n, self.sy.n))
        return len(self.lq) - 1

    def lq_emit_one(self):
        k = self.lq_emitted
        if self.ring_free_li[k % len(self.ring)] != k - len(self.ring):
            return False
        src_ap, scale_cols, width, ve_min, sy_min = self.lq[k]
        slot = self.ring[k % len(self.ring)]
        gate = self.ring_free[k % len(self.ring)]

        sctr = self.gsl[k % len(self.ring)]

        def j(g):
            if gate is not None:
                g.wait(gate[0], gate[1])
            dst = slot[:, :width]
            if src_ap.ndim == 3:
                dst = dst.rearrange("p (f w) -> p f w", w=src_ap.shape[2])
            sctr.sig(g.e.dma_start(out=dst, in_=src_ap))
        self.GP(j)
        rdy = (sctr, sctr.n)
        if scale_cols:
            def j2(v):
                v.wait(rdy[0], rdy[1])
                v.wait(self.ve, ve_min)
                v.e.wait_ge(self.sy.sem, sy_min)
                for colap, lo, hi in scale_cols:
                    v.tensor_scalar(slot[:, lo:hi], slot[:, lo:hi], colap,
                                    None, ALU.mult)
            self.VE(j2)
            rdy = (self.ve, self.ve.n)
        self.lq_ready.append(rdy)
        self.lq_emitted += 1
        return True

    def lq_need(self, li):
        while self.lq_emitted <= min(li, len(self.lq) - 1):
            if not self.lq_emit_one():
                break

    def w8(self, li):
        self.lq_need(li)
        assert li < self.lq_emitted, \
            "load %d blocked: slot %d not freed for %d" % (
                li, li % len(self.ring), li - len(self.ring))
        r = self.lq_ready[li]
        return (r[0], r[1], self.ring[li % len(self.ring)])

    def use(self, eng, li):
        assert li < self.lq_emitted, "load %d not emitted before body" % li
        eng.wait(*self.lq_ready[li])
        return self.ring[li % len(self.ring)]

    def free_slot(self, li, mark):
        self.ring_free[li % len(self.ring)] = mark
        self.ring_free_li[li % len(self.ring)] = li

    def done(self, li, prefetch=True):
        self.free_slot(li, (self.pe, self.pe.n))
        if prefetch:
            self.lq_need(li + len(self.ring))

    def n1cols(self, coltensor):
        return [(coltensor[:, c:c + 1], c * 128, (c + 1) * 128) for c in range(NKC)]

    def build(self):
        self.declare()
        with ExitStack() as st:
            self.block = st.enter_context(self.nc.Block())
            self.alloc(st)
            self.p0_consts()
            self.p1_input()
            self.p2_proj()
            self.p3_scan()
            self.p3b_cc_gate()
            self.p3c_corr()
            self.p3d_epilogue()
            self.p4_wo()
            self.p4b_ffn()
            self.p5_moe()
            self.p6_out()
        return self.nc

    def p0_consts(self):
        def jsy(s):
            for dst, src in [(self.cid, self.didf), (self.cidb, self.didb),
                             (self.cmask, self.dmask), (self.cones, self.dones),
                             (self.cgw, self.drtw), (self.cn1w, self.dn1w),
                             (self.cfnw, self.dfnw), (self.cgnw, self.dgnw),
                             (self.cgkb, self.dgkb), (self.csel, self.dsel)]:
                s.dma_start(out=dst[:], in_=src.ap())
            s.dma_start(out=self.cebias[0:1, :], in_=self.debias.ap())
        self.SY(jsy)
        self.c_rdy = self.sy.n

        def jg(g):
            g.dma_start(out=self.cgk1[:], in_=self.dgk1.ap())
            g.dma_start(out=self.cgk2[:], in_=self.dgk2.ap())
        self.GP(jg)
        self.smallw_rdy = self.gd.n

        def jv(v):
            v.wait(self.sy, self.c_rdy)
            v.tensor_scalar(self.cn1wq[:], self.cn1w[:], float(HK) ** -0.5,
                            None, ALU.mult)
        self.VE(jv)
        self.n1wq_rdy = self.ve.n

        def jt(t):
            t.wait(self.sy, self.c_rdy)
            t.matmul(self.pD[:, 0:self.E], self.cones[:], self.cebias[0:1, :],
                     start=True, stop=True)
        self.TE(jt)
        mmn = self.pe.n

        def jv2(v):
            v.wait(self.pe, mmn)
            v.tensor_copy(self.cebias[:], self.pD[:, 0:self.E])
        self.VE(jv2)
        self.consts_ve = self.ve.n

        self.li_wq = [self.lq_push(self.dwq.ap()[fb], self.n1cols(self.cn1wq))
                      for fb in range(KD // 128)]
        self.li_wk = [self.lq_push(self.dwk.ap()[fb], self.n1cols(self.cn1w))
                      for fb in range(KD // 128)]
        self.li_wv = [self.lq_push(self.dwv.ap()[fb], self.n1cols(self.cn1w))
                      for fb in range(VD // 128)]
        self.lq_need(len(self.ring) - 1)

    def p1_input(self):
        ntok, ntb = self.ntok, self.ntb
        xblk = self.REf[:, 0:2048]
        sqs = self.REf[:, 2048:4096]
        xnfmb = self.REf[:, 4096:6144]
        rr2row = self.RFf[0:1, 0:ntok]
        sm = self.smf
        self.spill_gd = {}
        prev_clear = None
        prev_ev = {}
        for tb in range(ntb):
            def jsy(s, tb=tb, prev=dict(prev_ev)):
                if tb == 0:
                    s.wait(self.sy, self.c_rdy)
                else:
                    s.wait(self.ve, prev["xnd"])
                    s.wait(self.se, prev["sq2"])
                    s.wait(self.pe, prev["trd"])
                    s.wait(self.sp, self.spill_gd[tb - 1])
                s.dma_start(out=xblk, in_=self.dx.ap()[tb * 128:(tb + 1) * 128, :])
            self.SY(jsy)
            xld = self.sy.n

            def jse(s, tb=tb, xld=xld):
                s.wait(self.sy, xld)
                s.activation(sqs, xblk, AF.Square, accum_out=sm[:, 0:1])
            self.SE(jse)
            sqd = self.se.n

            def jve(v, sqd=sqd, xld=xld):
                v.wait(self.sy, xld)
                v.tensor_reduce(sm[:, 1:2], xblk, AX.X, ALU.add)
                v.wait(self.se, sqd)
                v.wait(self.ve)
                v.tensor_scalar(sm[:, 2:3], sm[:, 1:2], 1.0 / D, None, ALU.mult)
                v.tensor_scalar(sm[:, 3:4], sm[:, 0:1], 1.0 / D, None, ALU.mult)
                v.wait(self.ve)
                v.scalar_tensor_tensor(sm[:, 4:5], sm[:, 2:3], sm[:, 2:3],
                                       sm[:, 3:4], ALU.mult, ALU.subtract)
                v.wait(self.ve)
                v.tensor_scalar(sm[:, 4:5], sm[:, 4:5], -1.0, EPS_LN,
                                ALU.mult, op1=ALU.add)
                v.wait(self.ve)
                v.reciprocal(sm[:, 5:6], sm[:, 4:5])
            self.VE(jve)
            vr = self.ve.n

            def jse2(s, vr=vr):
                s.wait(self.ve, vr)
                s.activation(sm[:, 6:7], sm[:, 5:6], AF.Sqrt)
            self.SE(jse2)
            rstd = self.se.n

            def jve2(v, rstd=rstd):
                v.wait(self.se, rstd)
                v.tensor_scalar(xblk, xblk, sm[:, 2:3], sm[:, 6:7],
                                ALU.subtract, op1=ALU.mult)
            self.VE(jve2)
            xnd = self.ve.n

            def jgp(g, tb=tb, xnd=xnd):
                g.wait(self.ve, xnd)
                self.sp.sig(g.e.dma_start(
                    out=self.dspill.ap()[tb * 128:(tb + 1) * 128, :], in_=xblk))
            self.GP(jgp)
            self.spill_gd[tb] = self.sp.n

            def jse3(s, xnd=xnd):
                s.wait(self.ve, xnd)
                s.activation(sqs, xblk, AF.Square, accum_out=sm[:, 7:8])
            self.SE(jse3)
            sq2 = self.se.n

            def jve3(v, sq2=sq2):
                v.wait(self.se, sq2)
                v.tensor_scalar(sm[:, 8:9], sm[:, 7:8], 1.0 / D, EPS_RMS,
                                ALU.mult, op1=ALU.add)
                v.wait(self.ve)
                v.reciprocal(sm[:, 9:10], sm[:, 8:9])
            self.VE(jve3)
            vr3 = self.ve.n

            def jse4(s, tb=tb, vr3=vr3):
                s.wait(self.ve, vr3)
                s.activation(self.rr2tm[:, tb:tb + 1], sm[:, 9:10], AF.Sqrt)
            self.SE(jse4)
            rr2d = self.se.n

            def jte(t, tb=tb, xnd=xnd, rr2d=rr2d, prev=dict(prev_ev)):
                t.wait(self.ve, xnd)
                if prev:
                    t.wait(self.ve, prev["xnfmd"])
                for c in range(NKC):
                    dst = (self.pA if c < 8 else self.pB)[:, (c % 8) * 128:(c % 8 + 1) * 128]
                    t.e.transpose(dst, xblk[:, c * 128:(c + 1) * 128], self.cid[:])
                t.wait(self.se, rr2d)
                t.transpose(self.pC[0:1, 0:128], self.rr2tm[:, tb:tb + 1], self.cid[:])
            self.TE(jte)
            trd = self.pe.n

            def jve4(v, tb=tb, trd=trd):
                v.wait(self.pe, trd)
                for c in range(NKC):
                    src = (self.pA if c < 8 else self.pB)[:, (c % 8) * 128:(c % 8 + 1) * 128]
                    v.tensor_copy(xnfmb[:, c * 128:(c + 1) * 128], src)
                    v.tensor_copy(self.RA[:, c * ntok + tb * 128:c * ntok + (tb + 1) * 128], src)
                v.tensor_copy(rr2row[0:1, tb * 128:(tb + 1) * 128], self.pC[0:1, 0:128])
            self.VE(jve4)
            xnfmd = self.ve.n

            def jte2(t, tb=tb, xnfmd=xnfmd, prev=dict(prev_ev)):
                t.wait(self.ve, xnfmd)
                if prev:
                    t.wait(self.se, prev["scd"])
                for c in range(NKC):
                    (t if c == NKC - 1 else t.e).matmul(
                        self.pD[:, 512:512 + self.E],
                        xnfmb[:, c * 128:(c + 1) * 128],
                        self.cgw[:, c * self.E:(c + 1) * self.E],
                        start=(c == 0), stop=(c == NKC - 1))
            self.TE(jte2)
            rlog = self.pe.n

            def jse5(s, tb=tb, rlog=rlog):
                s.wait(self.pe, rlog)
                if tb == 0:
                    s.wait(self.ve, self.consts_ve)
                s.activation(sm[:, 16:16 + self.E], self.pD[:, 512:512 + self.E],
                             AF.Sigmoid)
            self.SE(jse5)
            scd = self.se.n

            def jve6(v, tb=tb, scd=scd):
                E = self.E
                sc, bi, t1 = sm[:, 16:16 + E], sm[:, 32:32 + E], sm[:, 48:48 + E]
                v.wait(self.se, scd)
                v.tensor_tensor(bi, sc, self.cebias[:, 0:E], ALU.add)
                v.wait(self.ve)
                v.tensor_reduce(sm[:, 10:11], bi, AX.X, ALU.max)
                v.wait(self.ve)
                v.tensor_scalar(t1, bi, sm[:, 10:11], None, ALU.is_ge)
                v.wait(self.ve)
                v.scalar_tensor_tensor(bi, t1, -1e30, bi, ALU.mult, ALU.add)
                v.wait(self.ve)
                v.tensor_reduce(sm[:, 11:12], bi, AX.X, ALU.max)
                v.wait(self.ve)
                v.tensor_scalar(bi, bi, sm[:, 11:12], None, ALU.is_ge)
                v.wait(self.ve)
                v.tensor_tensor(t1, t1, bi, ALU.add)
                v.wait(self.ve)
                v.tensor_tensor(t1, t1, sc, ALU.mult)
                v.wait(self.ve)
                v.tensor_reduce(sm[:, 12:13], t1, AX.X, ALU.add)
                v.wait(self.ve)
                v.tensor_scalar(sm[:, 12:13], sm[:, 12:13], 1e-20, None, ALU.add)
                v.wait(self.ve)
                v.reciprocal(sm[:, 13:14], sm[:, 12:13])
                v.wait(self.ve)
                v.tensor_scalar(self.comb[:, tb * E:(tb + 1) * E], t1,
                                sm[:, 13:14], None, ALU.mult)
            self.VE(jve6)
            prev_clear = self.ve.n
            prev_ev = {"xnd": xnd, "sq2": sq2, "trd": trd,
                       "xnfmd": xnfmd, "scd": scd}
        self.p1_ve = self.ve.n

    def p2_proj(self):
        ntok, ntb = self.ntok, self.ntb
        qe = self.RD[:, 0:H * ntok]
        kc = self.RD[:, H * ntok:2 * H * ntok]
        rr2b = self.RFf[:, 1024:1024 + ntok]
        nth = max(1, ntok // 512)
        tw = min(512, ntok)

        for th in range(nth):
            def jt(t, th=th):
                t.wait(self.ve, self.p1_ve)
                if th > 0:
                    t.wait(self.ve, self.rrb_ev)
                t.matmul(self.pD[:, 0:tw], self.cones[:],
                         self.RFf[0:1, th * 512:th * 512 + tw], start=True, stop=True)
            self.TE(jt)
            mmn = self.pe.n

            def jv(v, th=th, mmn=mmn):
                v.wait(self.pe, mmn)
                v.tensor_copy(rr2b[:, th * 512:th * 512 + tw], self.pD[:, 0:tw])
            self.VE(jv)
            self.rrb_ev = self.ve.n
        rrb_rdy = self.ve.n

        psq = [self.pA[:, 0:512], self.pA[:, 512:1024],
               self.pB[:, 0:512], self.pB[:, 512:1024]]
        prev_ev = {}
        tile = 0
        for lis, dst in [(self.li_wq, qe), (self.li_wk, kc)]:
            for fb in range(H):
                for th in range(nth):
                    bank = psq[tile % 4]
                    key = tile % 4
                    tile += 1
                    rdy = self.w8(lis[fb])

                    def jt(t, fb=fb, th=th, lis=lis, bank=bank, key=key, rdy=rdy):
                        t.wait(rdy[0], rdy[1])
                        w = rdy[2]
                        if key in prev_ev:
                            t.wait(self.ve, prev_ev[key])
                        for c in range(NKC):
                            (t if c == NKC - 1 else t.e).matmul(
                                bank[:, 0:tw], w[:, c * 128:(c + 1) * 128],
                                self.xn_rhs(c, th), start=(c == 0),
                                stop=(c == NKC - 1))
                    self.TE(jt)
                    mmn = self.pe.n
                    if th == nth - 1:
                        self.done(lis[fb])

                    def jv(v, fb=fb, th=th, dst=dst, bank=bank, mmn=mmn):
                        v.wait(self.pe, mmn)
                        v.wait(self.ve, rrb_rdy)
                        v.tensor_tensor(
                            dst[:, fb * ntok + th * 512:fb * ntok + th * 512 + tw],
                            bank[:, 0:tw], rr2b[:, th * 512:th * 512 + tw], ALU.mult)
                    self.VE(jv)
                    prev_ev[key] = self.ve.n

        def jv_gk1(v):
            v.wait(self.gd, self.smallw_rdy)
            v.wait(self.sy, self.c_rdy)
            for c in range(NKC):
                v.tensor_scalar(self.cgk1[:, c * 16:(c + 1) * 16],
                                self.cgk1[:, c * 16:(c + 1) * 16],
                                self.cn1w[:, c:c + 1], None, ALU.mult)
        self.VE(jv_gk1)
        gk1s = self.ve.n

        glr_ev = {}
        for tb in range(ntb):
            def jt2(t, tb=tb):
                t.wait(self.ve, gk1s)
                if tb > 0:
                    t.wait(self.ve, glr_ev[tb - 1])
                for c in range(NKC):
                    (t if c == NKC - 1 else t.e).matmul(
                        self.pD[:, 512:528],
                        self.RA[:, c * ntok + tb * 128:c * ntok + (tb + 1) * 128],
                        self.cgk1[:, c * 16:(c + 1) * 16],
                        start=(c == 0), stop=(c == NKC - 1))
            self.TE(jt2)
            mmn = self.pe.n

            def jv2(v, tb=tb, mmn=mmn):
                v.wait(self.pe, mmn)
                v.tensor_scalar(self.glrt[:, tb * 16:(tb + 1) * 16],
                                self.pD[:, 512:528], self.rr2tm[:, tb:tb + 1],
                                None, ALU.mult)
            self.VE(jv2)
            glr_ev[tb] = self.ve.n
        glrt_rdy = self.ve.n

        for tb in range(ntb):
            sl = (tb % 2) * 512

            def jt4(t, tb=tb, sl=sl):
                t.wait(self.ve, glrt_rdy)
                if tb >= 2:
                    t.wait(self.ve, glr_ev[("t", tb - 2)])
                t.transpose(self.pD[0:16, sl:sl + 128],
                            self.glrt[:, tb * 16:(tb + 1) * 16], self.cid[:])
            self.TE(jt4)
            mmn = self.pe.n

            def jv4(v, tb=tb, sl=sl, mmn=mmn):
                v.wait(self.pe, mmn)
                v.tensor_copy(self.RF[0:16, 8448 + tb * 128:8448 + (tb + 1) * 128],
                              self.pD[0:16, sl:sl + 128])
            self.VE(jv4)
            glr_ev[("t", tb)] = self.ve.n
        glrf_rdy = self.ve.n

        scr = [(self.RFf[:, 2048:2560], self.RFf[:, 2560:3072], self.RFf[:, 3072:3584]),
               (self.RFf[:, 0:512], self.RFf[:, 512:1024], self.RFf[:, 3584:4096])]
        gb = [self.pD[:, 0:512], self.pD[:, 512:1024]]
        chain_hist = {}
        gjobs = [(kb, th) for kb in range(KD // 128) for th in range(nth)]
        vjobs = [(fb, th) for fb in range(VD // 128) for th in range(nth)]
        gi = 0

        def chain_tile(k):
            kb, th = gjobs[k]
            s1, s2, s3 = scr[k % 2]
            bank = gb[k % 2][:, 0:tw]

            def jt5(t, kb=kb, th=th, bank=bank, k=k):
                t.wait(self.ve, glrf_rdy)
                if k >= 2:
                    t.wait(self.ve, chain_hist[k - 2])
                t.matmul(bank, self.cgk2[:, kb * 128:(kb + 1) * 128],
                         self.RF[0:16, 8448 + th * 512:8448 + th * 512 + tw],
                         start=True, stop=True)
            self.TE(jt5)
            mmn = self.pe.n

            def jse(s, kb=kb, bank=bank, mmn=mmn, s1=s1, s2=s2):
                s.wait(self.pe, mmn)
                s.activation(s1[:, 0:tw], bank, AF.Identity,
                             bias=self.cgkb[:, kb:kb + 1])
                s.wait(self.se)
                s.activation(s2[:, 0:tw], s1[:, 0:tw], AF.Relu, scale=-1.0)
            self.SE(jse)
            sen = self.se.n

            def jve_(v, sen=sen, s1=s1, s2=s2, s3=s3):
                v.wait(self.se, sen)
                v.scalar_tensor_tensor(s3[:, 0:tw], s2[:, 0:tw], 2.0,
                                       s1[:, 0:tw], ALU.mult, ALU.add)
            self.VE(jve_)
            ven = self.ve.n

            def jse2(s, ven=ven, s3=s3):
                s.wait(self.ve, ven)
                s.activation(s3[:, 0:tw], s3[:, 0:tw], AF.Exp, scale=-1.0)
                s.wait(self.se)
                s.activation(s3[:, 0:tw], s3[:, 0:tw], AF.Ln, bias=1.0)
            self.SE(jse2)
            sen2 = self.se.n

            def jve2(v, kb=kb, th=th, sen2=sen2, s2=s2, s3=s3):
                v.wait(self.se, sen2)
                v.tensor_tensor(s2[:, 0:tw], s2[:, 0:tw], s3[:, 0:tw], ALU.add)
                v.wait(self.ve)
                for cc in range(tw // CH):
                    tok0 = th * 512 + cc * CH
                    v.tensor_tensor_scan(
                        self.RBf[:, kb * ntok + tok0:kb * ntok + tok0 + CH],
                        s2[:, cc * CH:(cc + 1) * CH],
                        s2[:, cc * CH:(cc + 1) * CH], 0.0, ALU.add, ALU.bypass)
            self.VE(jve2)
            chain_hist[k] = self.ve.n

        for vi, (fb, th) in enumerate(vjobs):
            bank = psq[tile % 4]
            key = tile % 4
            tile += 1
            rdy = self.w8(self.li_wv[fb])

            def jtv(t, fb=fb, th=th, bank=bank, key=key, rdy=rdy):
                t.wait(rdy[0], rdy[1])
                w = rdy[2]
                if key in prev_ev:
                    t.wait(self.ve, prev_ev[key])
                for c in range(NKC):
                    (t if c == NKC - 1 else t.e).matmul(
                        bank[:, 0:tw], w[:, c * 128:(c + 1) * 128],
                        self.xn_rhs(c, th), start=(c == 0),
                        stop=(c == NKC - 1))
            self.TE(jtv)
            mmn = self.pe.n
            if th == nth - 1:
                self.done(self.li_wv[fb])

            def jvv(v, th=th, bank=bank, mmn=mmn):
                v.wait(self.pe, mmn)
                v.wait(self.ve, rrb_rdy)
                if "vt_pe" in prev_ev:
                    v.wait(self.pe, prev_ev["vt_pe"])
                v.tensor_tensor(self.vts[:, 0:tw], bank[:, 0:tw],
                                rr2b[:, th * 512:th * 512 + tw], ALU.mult)
            self.VE(jvv)
            evn = self.ve.n

            def jtt(t, evn=evn):
                t.wait(self.ve, evn)
                if "vt_ve" in prev_ev:
                    t.wait(self.ve, prev_ev["vt_ve"])
                for tt in range(tw // 128):
                    (t if tt == tw // 128 - 1 else t.e).transpose(
                        self.pCb[:, tt * 128:(tt + 1) * 128],
                        self.vts[:, tt * 128:(tt + 1) * 128],
                        self.cidb[:])
            self.TE(jtt)
            prev_ev["vt_pe"] = self.pe.n
            ttn = self.pe.n

            def jvt(v, fb=fb, th=th, ttn=ttn):
                v.wait(self.pe, ttn)
                for tt in range(tw // 128):
                    tb = (th * 512 + tt * 128) // 128
                    v.tensor_copy(
                        self.RC[:, tb * 2048 + fb * 128:tb * 2048 + (fb + 1) * 128],
                        self.pCb[:, tt * 128:(tt + 1) * 128])
            self.VE(jvt)
            prev_ev["vt_ve"] = self.ve.n
            prev_ev[key] = self.ve.n

            while gi < len(gjobs) and gi * len(vjobs) < (vi + 1) * len(gjobs):
                chain_tile(gi)
                gi += 1
        while gi < len(gjobs):
            chain_tile(gi)
            gi += 1
        self.p2_ve = self.ve.n
        self.p2_se = self.se.n

    def xn_rhs(self, c, th):
        return self.RA[:, c * self.ntok + th * 512:c * self.ntok + th * 512 + min(512, self.ntok)]

    def p3_scan(self):
        ntok, nch = self.ntok, self.nch
        qe = self.RD[:, 0:H * ntok]
        kc = self.RD[:, H * ntok:2 * H * ntok]
        S = self.RFf[:, 0:2048]
        ep = self.RFf[:, 2048:2560]
        en = self.RFf[:, 2560:3072]
        blsc = self.RFf[:, 4096:4104]
        expbl = self.RFf[:, 4104:4112]
        kd = self.RF[:, 6144:6656]
        kutm = self.RF[:, 6656:7680]
        ATs = self.RF[:, 7680:8192]
        Sbf = self.RF[:, 8448:10496]
        obuf = self.RE
        cbh = self.RBf[:, 0:H * ntok].rearrange("p (h t) -> p h t", h=H)

        def jvS(v):
            v.wait(self.ve, self.p2_ve)
            v.e.wait_ge(self.sp.sem, self.spill_gd[self.ntb - 1])
            v.memset(S, 0.0)
            v.memset(Sbf, 0.0)
        self.VE(jvS)
        S_rdy = self.ve.n

        oev, Supd, ATev = {}, {}, {}
        for c in range(nch):
            tb, half = c // 2, c % 2
            prow = 64 * half
            tok0 = c * CH
            cb3 = cbh[:, :, tok0:tok0 + CH]

            def jv1(v, c=c):
                if c == 0:
                    v.wait(self.ve, self.p2_ve)
                    v.wait(self.se, self.p2_se)
                else:
                    v.wait(self.se, ATev[("se", c - 1)])
                v.tensor_scalar(blsc,
                                cbh[:, :, tok0 + CH - 1:tok0 + CH]
                                .rearrange("p h o -> p (h o)"),
                                -S16, None, ALU.mult)
            self.VE(jv1)
            v1 = self.ve.n

            def js1(s, c=c, v1=v1, cb3=cb3):
                s.wait(self.ve, v1)
                s.activation(expbl, blsc, AF.Exp)
                s.activation(ep.rearrange("p (h t) -> p h t", h=H), cb3,
                             AF.Exp, scale=-S16)
                s.activation(en.rearrange("p (h t) -> p h t", h=H), cb3,
                             AF.Exp, scale=S16)
            self.SE(js1)
            s1n = self.se.n
            ATev[("se", c)] = s1n

            qe3 = qe.rearrange("p (h t) -> p h t", h=H)[:, :, tok0:tok0 + CH]
            kc3 = kc.rearrange("p (h t) -> p h t", h=H)[:, :, tok0:tok0 + CH]

            def jv2(v, c=c, qe3=qe3, kc3=kc3, s1n=s1n):
                v.wait(self.se, s1n)
                if c >= 1:
                    v.wait(self.pe, ATev[("pe", c - 1)])
                v.tensor_tensor(qe3, qe3, ep.rearrange("p (h t) -> p h t", h=H),
                                ALU.mult)
                v.tensor_tensor(kd.rearrange("p (h t) -> p h t", h=H), kc3,
                                en.rearrange("p (h t) -> p h t", h=H), ALU.mult)
            self.VE(jv2)
            v2 = self.ve.n

            def jt1(t, c=c, prow=prow, tok0=tok0, v2=v2):
                t.wait(self.ve, v2)
                if c >= 1:
                    t.wait(self.ve, ATev[c - 1])
                for h in range(H):
                    (t if h == H - 1 else t.e).matmul(
                        self.pC[prow:prow + 64, h * 64:(h + 1) * 64],
                        kd[:, h * 64:(h + 1) * 64],
                        qe[:, h * ntok + tok0:h * ntok + tok0 + CH],
                        start=(h == 0), stop=(h == H - 1))
            self.TE(jt1)
            t1n = self.pe.n

            def jv3(v, c=c, prow=prow, tok0=tok0, t1n=t1n):
                v.wait(self.pe, t1n)
                v.tensor_copy(self.RBf[:, tok0:tok0 + H],
                              cbh[:, :, tok0 + CH - 1:tok0 + CH]
                              .rearrange("p h o -> p (h o)"))
                v.wait(self.ve)
                v.tensor_tensor(ATs[prow:prow + 64, :],
                                self.pC[prow:prow + 64, 0:H * CH],
                                self.cmask[prow:prow + 64, :], ALU.mult)
                for h in range(H):
                    v.tensor_scalar(kd[:, h * 64:(h + 1) * 64],
                                    kd[:, h * 64:(h + 1) * 64],
                                    expbl[:, h:h + 1], None, ALU.mult)
            self.VE(jv3)
            ATev[c] = self.ve.n
            v3 = self.ve.n

            def jt2(t, c=c, prow=prow, v3=v3):
                t.wait(self.ve, v3)
                for h in range(H):
                    (t if h == H - 1 else t.e).transpose(
                        self.pCb[prow:prow + 64, 1024 + h * 128:1024 + (h + 1) * 128],
                        kd[:, h * 64:(h + 1) * 64], self.cidb[:])
            self.TE(jt2)
            t2n = self.pe.n
            ATev[("pe", c)] = t2n

            def jv4(v, c=c, prow=prow, t2n=t2n):
                v.wait(self.pe, t2n)
                v.tensor_copy(kutm[prow:prow + 64, :],
                              self.pCb[prow:prow + 64, 1024:2048])
            self.VE(jv4)
            kutd = self.ve.n

            def jt3(t, c=c, tb=tb, prow=prow, tok0=tok0, v2=v2):
                t.wait(self.ve, v2)
                t.wait(self.ve, S_rdy if c == 0 else Supd[c - 1])
                if c >= 1:
                    t.wait(self.ve, oev[c - 1])
                for h in range(H):
                    po = (self.pA if h < 4 else self.pB)[prow:prow + 64,
                                                         (h % 4) * 256:(h % 4 + 1) * 256]
                    t.e.matmul(po, qe[:, h * ntok + tok0:h * ntok + tok0 + CH],
                               Sbf[:, h * 256:(h + 1) * 256],
                               start=(h % 2 == 0), stop=False)
                t.wait(self.ve, ATev[c])
                for h in range(H):
                    po = (self.pA if h < 4 else self.pB)[prow:prow + 64,
                                                         (h % 4) * 256:(h % 4 + 1) * 256]
                    (t if h == H - 1 else t.e).matmul(
                        po, ATs[prow:prow + 64, h * 64:(h + 1) * 64],
                        self.RC[prow:prow + 64, tb * 2048 + h * 256:tb * 2048 + (h + 1) * 256],
                        start=False, stop=(h % 2 == 1))
            self.TE(jt3)
            t3n = self.pe.n

            def jv5(v, c=c, tb=tb, prow=prow, t3n=t3n):
                v.wait(self.pe, t3n)
                v.tensor_copy(obuf[prow:prow + 64, tb * 2048:tb * 2048 + 1024],
                              self.pA[prow:prow + 64, 0:1024])
                v.tensor_copy(obuf[prow:prow + 64, tb * 2048 + 1024:(tb + 1) * 2048],
                              self.pB[prow:prow + 64, 0:1024])
            self.VE(jv5)
            oev[c] = self.ve.n

            for rnd in range(2):
                def jt4(t, c=c, tb=tb, prow=prow, rnd=rnd, kutd=kutd):
                    t.wait(self.ve, kutd)
                    if rnd == 0 and c >= 1:
                        t.wait(self.ve, Supd[c - 1])
                    if rnd == 1:
                        t.wait(self.ve, Supd[(c, 0)])
                    for hh in range(4):
                        h = rnd * 4 + hh
                        (t if hh == 3 else t.e).matmul(
                            self.pD[:, hh * 256:(hh + 1) * 256],
                            kutm[prow:prow + 64, h * 128:(h + 1) * 128],
                            self.RC[prow:prow + 64, tb * 2048 + h * 256:tb * 2048 + (h + 1) * 256],
                            start=(hh % 2 == 0), stop=(hh % 2 == 1))
                self.TE(jt4)
                t4n = self.pe.n

                def jv6(v, c=c, rnd=rnd, t4n=t4n):
                    v.wait(self.pe, t4n)
                    v.wait(self.ve, oev[c])
                    for hh in range(4):
                        h = rnd * 4 + hh
                        v.scalar_tensor_tensor(S[:, h * 256:(h + 1) * 256],
                                               S[:, h * 256:(h + 1) * 256],
                                               expbl[:, h:h + 1],
                                               self.pD[:, hh * 256:(hh + 1) * 256],
                                               ALU.mult, ALU.add)
                self.VE(jv6)
                if rnd == 0:
                    Supd[(c, 0)] = self.ve.n
                else:
                    vcp = self.ve.n

                    def jv7(v, vcp=vcp):
                        v.wait(self.ve, vcp)
                        v.tensor_copy(Sbf, S)
                    self.VE(jv7)
                    Supd[c] = self.ve.n
        self.p3_ve = self.ve.n
        self.scan_Supd_last = Supd[nch - 1]

    def p3b_cc_gate(self):
        ntok, ntb, nch = self.ntok, self.ntb, self.nch
        S = self.RFf[:, 0:2048]
        bs = self.RFf[:, 4112:4120]
        bsr = self.RFf[0:8, 4416:4672]
        cumh = self.cumbl[:, 0:H * nch].rearrange("p (h cc) -> p h cc", h=H)

        def jv(v):
            v.wait(self.ve, self.p3_ve)
            blbs = self.RBf[:, 0:nch * CH].rearrange("p (cc w) -> p w cc", w=CH)
            for h in range(H):
                v.tensor_tensor_scan(self.cumbl[:, h * nch:(h + 1) * nch],
                                     blbs[:, h, :], blbs[:, h, :],
                                     0.0, ALU.add, ALU.bypass)
            v.wait(self.ve)
            v.tensor_copy(bs, cumh[:, :, nch - 1:nch].rearrange("p h o -> p (h o)"))
        self.VE(jv)
        bs_rdy = self.ve.n

        def jt(t):
            t.wait(self.ve, bs_rdy)
            t.transpose(self.pD[0:8, 0:128], bs, self.cid[:])
        self.TE(jt)
        bst = self.pe.n

        def jv3(v):
            v.wait(self.pe, bst)
            v.memset(bsr, 0.0)
            v.wait(self.ve)
            v.tensor_copy(bsr[0:8, 0:128], self.pD[0:8, 0:128])
        self.VE(jv3)
        bsr_rdy = self.ve.n

        def jg(g):
            g.e.wait_ge(self.ve.sem, bsr_rdy)
            for h in range(H):
                self.cd.sig(g.e.dma_start(
                    out=self.dccin.ap()[h * 128:(h + 1) * 128, :],
                    in_=S[:, h * 256:(h + 1) * 256]))
            self.cd.sig(g.e.dma_start(
                out=self.dccin.ap()[H * 128:H * 128 + 8, :], in_=bsr))
            g.e.wait_ge(self.cd.sem, self.cd.n)
            self.cc.sig(g.e.collective_compute(
                "AllGather", ALU.bypass, replica_groups=[list(range(NCORES))],
                ins=[self.dccin.ap()], outs=[self.dccout.ap()]))
        self.GP(jg)
        cc_done = self.cc.n

        rrg = self.RFf[:, 2048:2048 + ntok]
        rrr = self.RFf[0:1, 1024:1024 + ntok]
        rr_ev = {}
        stag_cd = self.cd.n
        for tb in range(ntb):
            def jtr(t, tb=tb):
                t.wait(self.ve, bsr_rdy)
                if tb == 0:
                    t.e.wait_ge(self.cd.sem, stag_cd)
                if tb >= 2:
                    t.wait(self.ve, rr_ev[tb - 2])
                t.transpose(self.pD[0:1, (tb % 2) * 512:(tb % 2) * 512 + 128],
                            self.rr2tm[:, tb:tb + 1], self.cid[:])
            self.TE(jtr)
            n = self.pe.n

            def jvr(v, tb=tb, n=n):
                v.wait(self.pe, n)
                v.tensor_copy(rrr[0:1, tb * 128:(tb + 1) * 128],
                              self.pD[0:1, (tb % 2) * 512:(tb % 2) * 512 + 128])
            self.VE(jvr)
            rr_ev[tb] = self.ve.n
        rrr_rdy = self.ve.n

        nth = max(1, ntok // 512)
        tw = min(512, ntok)
        for th in range(nth):
            def jtb(t, th=th):
                t.wait(self.ve, rrr_rdy)
                if ("rb", th - 1) in rr_ev:
                    t.wait(self.ve, rr_ev[("rb", th - 1)])
                t.matmul(self.pD[:, 0:tw], self.cones[:],
                         rrr[0:1, th * 512:th * 512 + tw], start=True, stop=True)
            self.TE(jtb)
            bn = self.pe.n

            def jvb(v, th=th, bn=bn):
                v.wait(self.pe, bn)
                v.tensor_copy(rrg[:, th * 512:th * 512 + tw], self.pD[:, 0:tw])
            self.VE(jvb)
            rr_ev[("rb", th)] = self.ve.n
        rrg_rdy = self.ve.n

        self.li_wg = [self.lq_push(self.dwg.ap()[fb], self.n1cols(self.cn1w))
                      for fb in range(VD // 128)]
        psq = [self.pA[:, 0:512], self.pA[:, 512:1024],
               self.pB[:, 0:512], self.pB[:, 512:1024]]
        prev = {}
        tile = 0
        for fb in range(VD // 128):
            for th in range(nth):
                bank = psq[tile % 4]
                key = tile % 4
                tile += 1

                rdy = self.w8(self.li_wg[fb])

                def jtg(t, fb=fb, th=th, bank=bank, key=key, rdy=rdy):
                    t.wait(rdy[0], rdy[1])
                    w = rdy[2]
                    if key in prev:
                        t.wait(self.ve, prev[key])
                    for c in range(NKC):
                        (t if c == NKC - 1 else t.e).matmul(
                            bank[:, 0:tw], w[:, c * 128:(c + 1) * 128],
                            self.xn_rhs(c, th), start=(c == 0),
                            stop=(c == NKC - 1))
                self.TE(jtg)
                mmn = self.pe.n
                if th == nth - 1:
                    self.done(self.li_wg[fb])

                def jvg(v, th=th, bank=bank, mmn=mmn):
                    v.wait(self.pe, mmn)
                    v.wait(self.ve, rrg_rdy)
                    if "gt_pe" in prev:
                        v.wait(self.pe, prev["gt_pe"])
                    v.tensor_tensor(self.vts[:, 0:tw], bank[:, 0:tw],
                                    rrg[:, th * 512:th * 512 + tw], ALU.mult)
                self.VE(jvg)
                evn = self.ve.n

                def jtgt(t, evn=evn):
                    t.wait(self.ve, evn)
                    if "gt_ve" in prev:
                        t.wait(self.ve, prev["gt_ve"])
                    for tt in range(tw // 128):
                        (t if tt == tw // 128 - 1 else t.e).transpose(
                            self.pCb[:, tt * 128:(tt + 1) * 128],
                            self.vts[:, tt * 128:(tt + 1) * 128],
                            self.cidb[:])
                self.TE(jtgt)
                prev["gt_pe"] = self.pe.n
                ttn = self.pe.n

                def jvgt(v, fb=fb, th=th, ttn=ttn):
                    v.wait(self.pe, ttn)
                    for tt in range(tw // 128):
                        tb = (th * 512 + tt * 128) // 128
                        v.tensor_copy(
                            self.RB[:, tb * 2048 + fb * 128:tb * 2048 + (fb + 1) * 128],
                            self.pCb[:, tt * 128:(tt + 1) * 128])
                self.VE(jvgt)
                prev["gt_ve"] = self.ve.n
                prev[key] = self.ve.n
        self.gate_ve = self.ve.n

        SJ = self.RFf[:, 2048:4096]
        SJ3 = SJ.rearrange("p (h v) -> p h v", h=H)
        bacc = self.RFf[:, 4120:4128]
        ebac = self.RFf[:, 4128:4136]
        bsj = self.RFf[:, 4136:4144]

        def jvz(v):
            v.wait(self.cc, cc_done)
            v.wait(self.ve, self.gate_ve)
            v.memset(S, 0.0)
            v.memset(bacc, 0.0)
        self.VE(jvz)
        zn = self.ve.n

        sin_ev = {}
        for jj in range(NCORES - 1, -1, -1):
            base = jj * self.CCR

            def jgl(g, jj=jj, base=base):
                g.e.wait_ge(self.ve.sem, zn if jj == NCORES - 1 else sin_ev[jj + 1])
                self.cd.sig(g.e.dma_start(
                    out=SJ3,
                    in_=self.dccout.ap()[base:base + H * 128, :].rearrange(
                        "(h k) v -> k h v", h=H)))
                self.cd.sig(g.e.dma_start(
                    out=bsj,
                    in_=self.dccout.ap()[base + H * 128:base + H * 128 + 8, 0:128]
                    .rearrange("h k -> k h"), allow_slow_non_contiguous=True))
            self.GP(jgl)
            ld = self.cd.n

            def jse(s, ld=ld):
                s.e.wait_ge(self.cd.sem, ld)
                s.activation(ebac, bacc, AF.Exp, scale=-S16)
            self.SE(jse)
            sen = self.se.n

            def jvj(v, jj=jj, sen=sen, ld=ld):
                v.wait(self.se, sen)
                v.e.wait_ge(self.cd.sem, ld)
                v.tensor_scalar(ebac, ebac, self.csel[:, jj:jj + 1], None, ALU.mult)
                v.wait(self.ve)
                for h in range(H):
                    v.scalar_tensor_tensor(S[:, h * 256:(h + 1) * 256],
                                           SJ[:, h * 256:(h + 1) * 256],
                                           ebac[:, h:h + 1],
                                           S[:, h * 256:(h + 1) * 256],
                                           ALU.mult, ALU.add)
                v.scalar_tensor_tensor(bacc, bsj, self.csel[:, jj:jj + 1],
                                       bacc, ALU.mult, ALU.add)
            self.VE(jvj)
            sin_ev[jj] = self.ve.n

        def jvcp(v):
            v.wait(self.ve, sin_ev[0])
            v.tensor_copy(self.RF[:, 8448:10496], S)
        self.VE(jvcp)
        self.sin_ve = self.ve.n

    def p3c_corr(self):
        ntok, nch = self.ntok, self.nch
        qe = self.RD[:, 0:H * ntok]
        S = self.RF[:, 8448:10496]
        eB = self.RFf[:, 4144:4152]
        q2 = self.RF[:, 6144:6656]
        obuf = self.RE
        cumh = self.cumbl[:, 0:H * nch].rearrange("p (h cc) -> p h cc", h=H)
        corr_mm, corr_ev, corr_q2 = {}, {}, {}
        for c in range(nch):
            tb, half = c // 2, c % 2
            prow = 64 * half
            tok0 = c * CH

            def jse(s, c=c):
                if c == 0:
                    s.wait(self.ve, self.sin_ve)
                else:
                    s.wait(self.ve, corr_q2[c - 1])
                    s.activation(eB, cumh[:, :, c - 1:c].rearrange("p h o -> p (h o)"),
                                 AF.Exp, scale=-S16)
            self.SE(jse)
            sen = self.se.n

            def jv(v, c=c, tok0=tok0, sen=sen):
                v.wait(self.se, sen)
                if c == 0:
                    v.wait(self.ve, self.sin_ve)
                    v.memset(eB, 1.0)
                    v.wait(self.ve)
                if c >= 1:
                    v.wait(self.pe, corr_mm[c - 1])
                for h in range(H):
                    v.tensor_scalar(q2[:, h * 64:(h + 1) * 64],
                                    qe[:, h * ntok + tok0:h * ntok + tok0 + CH],
                                    eB[:, h:h + 1], None, ALU.mult)
            self.VE(jv)
            vn = self.ve.n
            corr_q2[c] = vn

            pa, pb = (self.pA, self.pB) if c % 2 == 0 else (self.pC, self.pD)

            def jt(t, c=c, prow=prow, vn=vn, pa=pa, pb=pb):
                t.wait(self.ve, vn)
                if c >= 2:
                    t.wait(self.ve, corr_ev[c - 2])
                for h in range(H):
                    po = (pa if h < 4 else pb)[prow:prow + 64,
                                               (h % 4) * 256:(h % 4 + 1) * 256]
                    (t if h == H - 1 else t.e).matmul(
                        po, q2[:, h * 64:(h + 1) * 64],
                        S[:, h * 256:(h + 1) * 256],
                        start=(h % 2 == 0), stop=(h % 2 == 1))
            self.TE(jt)
            corr_mm[c] = self.pe.n
            tn = self.pe.n

            def jv2(v, c=c, tb=tb, prow=prow, tn=tn, pa=pa, pb=pb):
                v.wait(self.pe, tn)
                v.tensor_tensor(obuf[prow:prow + 64, tb * 2048:tb * 2048 + 1024],
                                obuf[prow:prow + 64, tb * 2048:tb * 2048 + 1024],
                                pa[prow:prow + 64, 0:1024], ALU.add)
                v.tensor_tensor(obuf[prow:prow + 64, tb * 2048 + 1024:(tb + 1) * 2048],
                                obuf[prow:prow + 64, tb * 2048 + 1024:(tb + 1) * 2048],
                                pb[prow:prow + 64, 0:1024], ALU.add)
            self.VE(jv2)
            corr_ev[c] = self.ve.n
        self.corr_ve = self.ve.n

    def p3d_epilogue(self):
        ntok, ntb = self.ntok, self.ntb
        obuf = self.RE
        gate = self.RB
        sq = self.RFf[:, 0:2048]
        silu = self.RF[:, 4096:6144]
        ogs = self.RF[:, 6144:8192]
        rr4 = self.RFf[:, 4152:4160]
        epi_ev = {}
        for tb in range(ntb):
            def jse(s, tb=tb):
                if tb == 0:
                    s.wait(self.ve, self.corr_ve)
                else:
                    s.wait(self.ve, epi_ev[tb - 1])
                s.activation(sq, obuf[:, tb * 2048:(tb + 1) * 2048], AF.Square)
                s.wait(self.se)
                s.activation(silu, gate[:, tb * 2048:(tb + 1) * 2048], AF.Silu)
            self.SE(jse)
            sen = self.se.n

            def jv(v, sen=sen):
                v.wait(self.se, sen)
                v.tensor_reduce(rr4, sq.rearrange("p (h v) -> p h v", h=H),
                                AX.X, ALU.add)
                v.wait(self.ve)
                v.tensor_scalar(rr4, rr4, 1.0 / HV, EPS_RMS, ALU.mult, op1=ALU.add)
                v.wait(self.ve)
                v.reciprocal(rr4, rr4)
            self.VE(jv)
            vn = self.ve.n

            def jse2(s, vn=vn):
                s.wait(self.ve, vn)
                s.activation(rr4, rr4, AF.Sqrt)
            self.SE(jse2)
            sen2 = self.se.n

            def jv2(v, tb=tb, sen2=sen2):
                v.wait(self.se, sen2)
                for h in range(H):
                    v.tensor_scalar(ogs[:, h * 256:(h + 1) * 256],
                                    obuf[:, tb * 2048 + h * 256:tb * 2048 + (h + 1) * 256],
                                    rr4[:, h:h + 1], None, ALU.mult)
                v.wait(self.ve)
                v.tensor_tensor(ogs, ogs, silu, ALU.mult)
            self.VE(jv2)
            vn2 = self.ve.n

            def jt(t, tb=tb, vn2=vn2):
                t.wait(self.ve, vn2)
                if tb >= 1:
                    t.wait(self.ve, epi_ev[tb - 1])
                for c in range(NKC):
                    dst = (self.pCb if c < 8 else self.pDb)[:, (c % 8) * 128:(c % 8 + 1) * 128]
                    (t if c == NKC - 1 else t.e).transpose(
                        dst, ogs[:, c * 128:(c + 1) * 128], self.cidb[:])
            self.TE(jt)
            tn = self.pe.n

            def jv3(v, tb=tb, tn=tn):
                v.wait(self.pe, tn)
                for c in range(NKC):
                    src = (self.pCb if c < 8 else self.pDb)[:, (c % 8) * 128:(c % 8 + 1) * 128]
                    v.tensor_copy(self.RC[:, c * ntok + tb * 128:c * ntok + (tb + 1) * 128], src)
            self.VE(jv3)
            epi_ev[tb] = self.ve.n
        self.epi_ev = epi_ev
        self.epi_ve = self.ve.n

    def p4_wo(self):
        ntok, ntb = self.ntok, self.ntb
        og = self.RC
        acc = self.RE
        xnt = self.RF[:, 8448:10496]
        h2t = self.RF[:, 2048:4096]
        sqsc = self.RFf[:, 2048:2560]
        sm = self.smf
        banks4 = [self.pA[:, 0:512], self.pA[:, 512:1024],
                  self.pB[:, 0:512], self.pB[:, 512:1024]]
        wo_ev = {}
        for ds in range(4):
            li4 = [self.lq_push(
                self.dwo.ap()[ds, g4],
                scale_cols=[(self.cgnw[:, ((g4 * 4 + f) % 2):((g4 * 4 + f) % 2) + 1],
                             f * 512, (f + 1) * 512) for f in range(4)])
                for g4 in range(NKC // 4)]
            rdy4 = [self.w8(li) for li in li4]
            for tb in range(ntb):
                def jsy(s, ds=ds, tb=tb):
                    s.wait(self.ve, self.epi_ev[tb])
                    if (ds - 1, tb) in wo_ev:
                        s.wait(self.ve, wo_ev[(ds - 1, tb)])
                    if tb > 0:
                        s.wait(self.ve, wo_ev[(ds, tb - 1)])
                    s.wait(self.sp, self.spill_gd[tb])
                    s.wait(self.sy)
                    s.dma_start(out=xnt[:, ds * 512:(ds + 1) * 512],
                                in_=self.dspill.ap()[tb * 128:(tb + 1) * 128,
                                                     ds * 512:(ds + 1) * 512])
                self.SY(jsy)
                xld = self.sy.n
                bank = banks4[tb % 4]

                def jt(t, ds=ds, tb=tb, bank=bank, rdy4=rdy4):
                    t.wait(self.ve, self.epi_ev[tb])
                    if tb >= 4:
                        t.wait(self.ve, wo_ev[(ds, tb - 4)])
                    elif (ds - 1, tb + 4) in wo_ev:
                        t.wait(self.ve, wo_ev[(ds - 1, tb + 4)])
                    for g4 in range(NKC // 4):
                        t.wait(rdy4[g4][0], rdy4[g4][1])
                        w = rdy4[g4][2]
                        for f in range(4):
                            c = g4 * 4 + f
                            sign = (c == NKC - 1) or (tb == ntb - 1 and f == 3)
                            (t if sign else t.e).matmul(
                                bank,
                                og[:, c * ntok + tb * 128:c * ntok + (tb + 1) * 128],
                                w[:, f * 512:(f + 1) * 512],
                                start=(c == 0), stop=(c == NKC - 1))
                        if tb == ntb - 1:
                            self._womk[g4] = self.pe.n
                self.TE(jt)
                if tb == 0:
                    self._womk = {}
                mmn = self.pe.n

                def jv(v, ds=ds, tb=tb, bank=bank, mmn=mmn, xld=xld):
                    v.wait(self.pe, mmn)
                    v.e.wait_ge(self.sy.sem, xld)
                    v.tensor_tensor(acc[:, tb * 2048 + ds * 512:tb * 2048 + (ds + 1) * 512],
                                    bank, xnt[:, ds * 512:(ds + 1) * 512], ALU.add)
                self.VE(jv)
                wo_ev[(ds, tb)] = self.ve.n
            for g4, li in enumerate(li4):
                self.free_slot(li, (self.pe, self._womk[g4]))
                self.lq_need(li + len(self.ring))
        acc_all = self.ve.n

        for tb in range(ntb):
            def jse(s, tb=tb):
                s.wait(self.ve, acc_all if tb == 0 else wo_ev[("h2", tb - 1)])
                for ds in range(4):
                    if ds:
                        s.wait(self.se)
                    s.activation(sqsc, acc[:, tb * 2048 + ds * 512:tb * 2048 + (ds + 1) * 512],
                                 AF.Square, accum_out=sm[:, 20 + ds:21 + ds])
            self.SE(jse)
            sen = self.se.n

            def jv2(v, sen=sen):
                v.wait(self.se, sen)
                v.tensor_reduce(sm[:, 24:25], sm[:, 20:24], AX.X, ALU.add)
                v.wait(self.ve)
                v.tensor_scalar(sm[:, 24:25], sm[:, 24:25], 1.0 / D, EPS_RMS,
                                ALU.mult, op1=ALU.add)
                v.wait(self.ve)
                v.reciprocal(sm[:, 25:26], sm[:, 24:25])
            self.VE(jv2)
            vn = self.ve.n

            def jse2(s, vn=vn):
                s.wait(self.ve, vn)
                s.activation(sm[:, 26:27], sm[:, 25:26], AF.Sqrt)
            self.SE(jse2)
            sen2 = self.se.n

            def jv3(v, tb=tb, sen2=sen2):
                s = sen2
                v.wait(self.se, sen2)
                if tb >= 1:
                    v.wait(self.pe, wo_ev[("h2pe", tb - 1)])
                v.tensor_scalar(h2t, acc[:, tb * 2048:(tb + 1) * 2048],
                                sm[:, 26:27], None, ALU.mult)
            self.VE(jv3)
            vn3 = self.ve.n

            def jt2(t, vn3=vn3, tb=tb):
                t.wait(self.ve, vn3)
                if tb >= 1:
                    t.wait(self.ve, wo_ev[("h2", tb - 1)])
                for c in range(NKC):
                    (t if c == NKC - 1 else t.e).transpose(
                        self.pDb[:, c * 128:(c + 1) * 128],
                        h2t[:, c * 128:(c + 1) * 128], self.cidb[:])
            self.TE(jt2)
            wo_ev[("h2pe", tb)] = self.pe.n
            tn = self.pe.n

            def jv4(v, tb=tb, tn=tn):
                v.wait(self.pe, tn)
                for c in range(NKC):
                    v.tensor_scalar(
                        self.RD[:, c * ntok + tb * 128:c * ntok + (tb + 1) * 128],
                        self.pDb[:, c * 128:(c + 1) * 128],
                        self.cfnw[:, c:c + 1], None, ALU.mult)
            self.VE(jv4)
            wo_ev[("h2", tb)] = self.ve.n
        self.p4_ve = self.ve.n

    def p4b_ffn(self):
        ntok, ntb, NFC = self.ntok, self.ntb, self.NFC
        h2 = self.RD
        acc = self.RE
        sg = self.RF[:, 4096:6144]
        banks = [self.pA[:, 0:512], self.pA[:, 512:1024],
                 self.pB[:, 0:512], self.pB[:, 512:1024]]
        nth = max(1, ntok // 512)
        tw = min(512, ntok)
        upw = min(1024, ntok)
        npass = max(1, ntb // 4)
        ta = min(16, ((NFC + 11) // 12) * 4)
        tb_ = min(32, 2 * ta)
        thirds = [t for t in [(0, ta), (ta, tb_), (tb_, NFC)] if t[1] > t[0]]
        prev = {"dn_ev": self.p4_ve}

        def mid_of(t3):
            return self.RB if t3 % 2 == 0 else self.RC

        def up_unit(t3, jloc, lis):
            mid = mid_of(t3)
            bga, bgb = self.pC, self.pD
            rg_ = self.w8(lis[jloc][0])
            ru_ = self.w8(lis[jloc][1])

            def jt(t, t3=t3, jloc=jloc, rg_=rg_, ru_=ru_, bga=bga, bgb=bgb):
                if jloc == 0 and t3 == 0:
                    t.wait(self.ve, self.p4_ve)
                    t.wait(self.ve, prev["dn_ev"])
                if jloc == 0 and t3 >= 2:
                    t.wait(self.ve, prev[("dn_done", t3 - 2)])
                if "se" in prev:
                    t.wait(self.se, prev["se"])
                t.wait(rg_[0], rg_[1])
                w = rg_[2]
                for th in range(nth):
                    for c in range(NKC):
                        (t if c == NKC - 1 and th == nth - 1 else t.e).matmul(
                            bga[:, th * 512:th * 512 + tw],
                            w[:, c * 128:(c + 1) * 128],
                            h2[:, c * ntok + th * 512:c * ntok + th * 512 + tw],
                            start=(c == 0), stop=(c == NKC - 1))
                self._gmark = self.pe.n
                if "a" in prev:
                    t.wait(self.ve, prev["a"])
                t.wait(ru_[0], ru_[1])
                w = ru_[2]
                for th in range(nth):
                    for c in range(NKC):
                        (t if c == NKC - 1 and th == nth - 1 else t.e).matmul(
                            bgb[:, th * 512:th * 512 + tw],
                            w[:, c * 128:(c + 1) * 128],
                            h2[:, c * ntok + th * 512:c * ntok + th * 512 + tw],
                            start=(c == 0), stop=(c == NKC - 1))
            self.TE(jt)
            mmn = self.pe.n
            self.free_slot(lis[jloc][0], (self.pe, self._gmark))
            self.lq_need(lis[jloc][0] + len(self.ring))
            self.done(lis[jloc][1])

            def jse(s, mmn=mmn, bga=bga):
                s.wait(self.pe, mmn)
                s.activation(sg[:, 0:upw], bga[:, 0:upw], AF.Silu)
            self.SE(jse)
            sen = self.se.n
            prev["se"] = sen

            def jv(v, t3=t3, jloc=jloc, sen=sen, mid=mid, bgb=bgb):
                v.wait(self.se, sen)
                v.tensor_tensor(mid[:, jloc * ntok:jloc * ntok + upw],
                                sg[:, 0:upw], bgb[:, 0:upw], ALU.mult)
            self.VE(jv)
            prev["a"] = self.ve.n

        def dn_unit(t3, ds, ps, lis2, mid_rdy, nfc_loc):
            mid = mid_of(t3)
            tbs = list(range(ps * 4, min(ps * 4 + 4, ntb)))

            def jt(t, t3=t3, ds=ds, lis2=lis2, mid_rdy=mid_rdy,
                   nfc_loc=nfc_loc, tbs=tbs):
                t.wait(self.ve, mid_rdy)
                t.wait(self.ve, prev["dn_ev"])
                for (i0, gn, li) in lis2:
                    t.wait(*self.lq_ready[li])
                    w = self.ring[li % len(self.ring)]
                    for f in range(gn):
                        i = i0 + f
                        for tb in tbs:
                            ((t if tb == tbs[-1] and f == gn - 1 else t.e)
                             .matmul)(banks[tb % 4],
                                      mid[:, i * ntok + tb * 128:i * ntok + (tb + 1) * 128],
                                      w[:, f * 512:(f + 1) * 512],
                                      start=(i == 0), stop=(i == nfc_loc - 1))
            self.TE(jt)
            mmn = self.pe.n

            def jv(v, ds=ds, mmn=mmn, tbs=tbs):
                v.wait(self.pe, mmn)
                for tb in tbs:
                    sl = acc[:, tb * 2048 + ds * 512:tb * 2048 + (ds + 1) * 512]
                    v.tensor_tensor(sl, sl, banks[tb % 4], ALU.add)
            self.VE(jv)
            prev["dn_ev"] = self.ve.n

        def push_dn(t3, ds):
            f0, f1 = thirds[t3]
            return [(g * 4 - f0, min(4, f1 - g * 4),
                     self.lq_push(self.dwdn.ap()[ds, g]))
                    for g in range(f0 // 4, (f1 + 3) // 4)]

        dn_pend = None
        for t3, (f0, f1) in enumerate(thirds):
            nfc_loc = f1 - f0
            lis = []
            lis2_cache = {}
            dn_units = []
            if dn_pend is not None:
                pt3, pmid, pnfc = dn_pend
                dn_units = [(pt3, ds, ps, pmid, pnfc)
                            for ds in range(4) for ps in range(npass)]
            nup = nfc_loc
            di = 0
            for ui in range(nup):
                lis.append((self.lq_push(self.dwgate.ap()[f0 + ui]),
                            self.lq_push(self.dwup.ap()[f0 + ui])))
                up_unit(t3, ui, lis)
                while di < len(dn_units) and (di + 1) * nup <= (ui + 1) * len(dn_units):
                    pt3, ds, ps, pmid, pnfc = dn_units[di]
                    lis2 = push_dn(pt3, ds)
                    for (i0, gn, li) in lis2:
                        self.w8(li)
                    dn_unit(pt3, ds, ps, lis2, pmid, pnfc)
                    for (i0, gn, li) in lis2:
                        self.done(li)
                    di += 1
            while di < len(dn_units):
                pt3, ds, ps, pmid, pnfc = dn_units[di]
                lis2 = push_dn(pt3, ds)
                for (i0, gn, li) in lis2:
                    self.w8(li)
                dn_unit(pt3, ds, ps, lis2, pmid, pnfc)
                for (i0, gn, li) in lis2:
                    self.done(li)
                di += 1
            if dn_pend is not None:
                prev[("dn_done", dn_pend[0])] = self.ve.n
            dn_pend = (t3, self.ve.n, nfc_loc)

        pt3, pmid, pnfc = dn_pend
        lis2_cache = {}
        for ds in range(4):
            for ps in range(npass):
                if ps == 0:
                    lis2_cache[ds] = push_dn(pt3, ds)
                    for (i0, gn, li) in lis2_cache[ds]:
                        self.w8(li)
                dn_unit(pt3, ds, ps, lis2_cache[ds], pmid, pnfc)
                if ps == npass - 1:
                    for (i0, gn, li) in lis2_cache[ds]:
                        self.done(li)
        self.p4b_ve = self.ve.n

    def p5_moe(self):
        ntok, ntb, E, FSH = self.ntok, self.ntb, self.E, self.FSH
        acc = self.RE
        sg = self.RF[:, 4096:6144]
        banks = [self.pA[:, 0:512], self.pA[:, 512:1024],
                 self.pB[:, 0:512], self.pB[:, 512:1024],
                 self.pC[:, 0:512], self.pC[:, 512:1024],
                 self.pD[:, 0:512], self.pD[:, 512:1024]]
        nth = max(1, ntok // 512)
        tw = min(512, ntok)
        npass = max(1, ntb // 4)
        prev = {"dn_ev": self.p4b_ve, "up_done": {}}

        def midslot(ei):
            return self.RD[:, (ei % 2) * FSH * ntok:((ei % 2) + 1) * FSH * ntok]

        def up_srcs(ei):
            if ei == 0:
                return [(self.dsgw.ap()[j], self.dsuw.ap()[j]) for j in range(FSH)]
            e = ei - 1
            return [(self.degu.ap()[e, j], self.degu.ap()[e, FSH + j])
                    for j in range(FSH)]

        def dn_src(ei, ds, g):
            if ei == 0:
                return self.dsdw.ap()[ds, g]
            return self.ded.ap()[ei - 1, ds, g]

        def comb_col(ei, tb):
            if ei == 0:
                return None
            e = ei - 1
            return self.comb[:, tb * E + e:tb * E + e + 1]

        upw = min(1024, ntok)

        def up_unit(ei, j, lis):
            mid = midslot(ei)
            bga, bgb = self.pC, self.pD
            rg_ = self.w8(lis[j][0])
            ru_ = self.w8(lis[j][1])

            def jt(t, ei=ei, j=j, rg_=rg_, ru_=ru_, bga=bga, bgb=bgb):
                if j == 0 and ei == 0:
                    t.wait(self.ve, prev["dn_ev"])
                if j == 0 and ei >= 2:
                    t.wait(self.ve, prev[("dn_done", ei - 2)])
                if "se" in prev:
                    t.wait(self.se, prev["se"])
                t.wait(rg_[0], rg_[1])
                w = rg_[2]
                for th in range(nth):
                    for c in range(NKC):
                        (t if c == NKC - 1 and th == nth - 1 else t.e).matmul(
                            bga[:, th * 512:th * 512 + tw],
                            w[:, c * 128:(c + 1) * 128],
                            self.xn_rhs(c, th), start=(c == 0), stop=(c == NKC - 1))
                self._gmark = self.pe.n
                if "a" in prev:
                    t.wait(self.ve, prev["a"])
                t.wait(ru_[0], ru_[1])
                w = ru_[2]
                for th in range(nth):
                    for c in range(NKC):
                        (t if c == NKC - 1 and th == nth - 1 else t.e).matmul(
                            bgb[:, th * 512:th * 512 + tw],
                            w[:, c * 128:(c + 1) * 128],
                            self.xn_rhs(c, th), start=(c == 0), stop=(c == NKC - 1))
            self.TE(jt)
            mmn = self.pe.n
            self.free_slot(lis[j][0], (self.pe, self._gmark))
            self.lq_need(lis[j][0] + len(self.ring))
            self.done(lis[j][1])

            def jse(s, mmn=mmn, bga=bga):
                s.wait(self.pe, mmn)
                s.activation(sg[:, 0:upw], bga[:, 0:upw], AF.Silu)
            self.SE(jse)
            sen = self.se.n
            prev["se"] = sen

            def jv(v, j=j, sen=sen, mid=mid, bgb=bgb):
                v.wait(self.se, sen)
                v.tensor_tensor(mid[:, j * ntok:j * ntok + upw],
                                sg[:, 0:upw], bgb[:, 0:upw], ALU.mult)
            self.VE(jv)
            prev["a"] = self.ve.n

        def dn_unit(ei, ds, ps, lis2, mid_rdy):
            mid = midslot(ei)
            tbs = list(range(ps * 4, min(ps * 4 + 4, ntb)))

            def jt(t, ei=ei, ds=ds, ps=ps, lis2=lis2, mid_rdy=mid_rdy, tbs=tbs):
                t.wait(self.ve, mid_rdy)
                t.wait(self.ve, prev["dn_ev"])
                for (i0, gn, li) in lis2:
                    t.wait(*self.lq_ready[li])
                    w = self.ring[li % len(self.ring)]
                    for f in range(gn):
                        i = i0 + f
                        for tb in tbs:
                            ((t if tb == tbs[-1] and f == gn - 1 else t.e)
                             .matmul)(banks[tb % 4],
                                      mid[:, i * ntok + tb * 128:i * ntok + (tb + 1) * 128],
                                      w[:, f * 512:(f + 1) * 512],
                                      start=(i == 0), stop=(i == FSH - 1))
            self.TE(jt)
            mmn = self.pe.n

            def jv(v, ei=ei, ds=ds, mmn=mmn, tbs=tbs):
                v.wait(self.pe, mmn)
                for tb in tbs:
                    col = comb_col(ei, tb)
                    sl = acc[:, tb * 2048 + ds * 512:tb * 2048 + (ds + 1) * 512]
                    if col is None:
                        v.tensor_tensor(sl, sl, banks[tb % 4], ALU.add)
                    else:
                        v.scalar_tensor_tensor(sl, banks[tb % 4], col, sl,
                                               ALU.mult, ALU.add)
            self.VE(jv)
            prev["dn_ev"] = self.ve.n

        def push_dn_group(ei, ds):
            return [(g * 4, 4, self.lq_push(dn_src(ei, ds, g)))
                    for g in range(FSH // 4)]

        def run_dn(pei, ds, ps, lis2_cache, pmid):
            if ps == 0:
                lis2_cache[ds] = push_dn_group(pei, ds)
                for (i0, gn, li) in lis2_cache[ds]:
                    self.w8(li)
            dn_unit(pei, ds, ps, lis2_cache[ds], pmid)
            if ps == npass - 1:
                for (i0, gn, li) in lis2_cache[ds]:
                    self.done(li)

        dn_pend = None
        for ei in range(E + 1):
            srcs = up_srcs(ei)
            lis = []
            upj = list(range(FSH))
            dn_units = []
            lis2_cache = {}
            if dn_pend is not None:
                pei, pmid = dn_pend
                dn_units = [(pei, ds, ps, pmid)
                            for ds in range(4) for ps in range(npass)]
            di = 0
            for ui, j in enumerate(upj):
                lis.append((self.lq_push(srcs[j][0]), self.lq_push(srcs[j][1])))
                up_unit(ei, j, lis)
                while di < len(dn_units) and (di + 1) * len(upj) <= (ui + 1) * len(dn_units):
                    pei, ds, ps, pmid = dn_units[di]
                    run_dn(pei, ds, ps, lis2_cache, pmid)
                    di += 1
            while di < len(dn_units):
                pei, ds, ps, pmid = dn_units[di]
                run_dn(pei, ds, ps, lis2_cache, pmid)
                di += 1
            if dn_pend is not None:
                prev[("dn_done", dn_pend[0])] = self.ve.n
            mid_rdy = self.ve.n
            dn_pend = (ei, mid_rdy)

        pei, pmid = dn_pend
        lis2_cache = {}
        for ds in range(4):
            for ps in range(npass):
                run_dn(pei, ds, ps, lis2_cache, pmid)
        self.p5_ve = self.ve.n

    def p6_out(self):
        ntb = self.ntb
        acc = self.RE
        slots = [self.RFf[:, 0:2048], self.RFf[:, 2048:4096]]
        st_ev = {}
        for tb in range(ntb):
            sl = slots[tb % 2]

            def jsy(s, tb=tb, sl=sl):
                if tb < 2:
                    s.wait(self.ve, self.p5_ve)
                else:
                    s.e.wait_ge(self.so.sem, st_ev[tb - 2])
                s.dma_start(out=sl, in_=self.dx.ap()[tb * 128:(tb + 1) * 128, :])
            self.SY(jsy)
            xld = self.sy.n

            def jv(v, tb=tb, sl=sl, xld=xld):
                v.e.wait_ge(self.sy.sem, xld)
                if tb == 0:
                    v.wait(self.ve, self.p5_ve)
                v.tensor_tensor(sl, acc[:, tb * 2048:(tb + 1) * 2048], sl,
                                ALU.add)
            self.VE(jv)
            vn = self.ve.n

            def jgp(g, tb=tb, sl=sl, vn=vn):
                g.e.wait_ge(self.ve.sem, vn)
                self.so.sig(g.e.dma_start(
                    out=self.dout.ap()[tb * 128:(tb + 1) * 128, :], in_=sl))
            self.GP(jgp)
            st_ev[tb] = self.so.n

        def jfin(s):
            s.e.wait_ge(self.so.sem, st_ev[ntb - 1])
        self.SY(jfin)


def build_program(cfg):
    return MK(cfg).build()


def _padc(w, m=512):
    F = w.shape[1]
    pad = (-F) % m
    if pad == 0:
        return w
    return np.concatenate([w, np.zeros((w.shape[0], pad), w.dtype)], axis=1)


def _padr(w, m=512):
    F = w.shape[0]
    pad = (-F) % m
    if pad == 0:
        return w
    return np.concatenate([w, np.zeros((pad, w.shape[1]), w.dtype)], axis=0)


def _tile_up(w):
    Dd, F = w.shape
    return np.ascontiguousarray(
        w.reshape(Dd // 128, 128, F // 128, 128).transpose(2, 1, 0, 3)
        .reshape(F // 128, 128, Dd))


def _tile_down(w):
    F, Dout = w.shape
    assert F % 512 == 0
    return np.ascontiguousarray(
        w.reshape(F // 512, 4, 128, 4, Dout // 4).transpose(3, 0, 2, 1, 4)
        .reshape(4, F // 512, 128, 4 * (Dout // 4)))


def kernel(**inputs):
    cfg = dict(ntb=8, E=16, nfc=44, fsh=8)
    ntok = cfg["ntb"] * 128

    x = np.asarray(inputs["L0_attn_x"], dtype=np.float32)
    B, T, _ = x.shape
    nseg = NCORES // B

    f32 = lambda a: np.asarray(a, dtype=np.float32)
    shared = dict(
        wq=_tile_up(f32(inputs["L0_attn_wq"])),
        wk=_tile_up(f32(inputs["L0_attn_wk"])),
        wv=_tile_up(f32(inputs["L0_attn_wv"])),
        wg=_tile_up(f32(inputs["L0_attn_wg"])),
        gk1=np.ascontiguousarray(
            f32(inputs["L0_attn_gk_w1"]).reshape(NKC, 128, 16)
            .transpose(1, 0, 2).reshape(128, NKC * 16)),
        gk2=np.ascontiguousarray(f32(inputs["L0_attn_gk_w2"])),
        gkb=np.ascontiguousarray(f32(inputs["L0_attn_gk_b"]).reshape(8, 128).T),
        wo=_tile_down(f32(inputs["L0_attn_wo"])),
        n1w=np.ascontiguousarray(f32(inputs["L0_attn_norm1_w"]).reshape(NKC, 128).T),
        fnw=np.ascontiguousarray(f32(inputs["L0_attn_ffn_norm_w"]).reshape(NKC, 128).T),
        gnw=np.ascontiguousarray(f32(inputs["L0_attn_gnorm_w"]).reshape(2, 128).T),
        wgate=_tile_up(_padc(f32(inputs["L0_attn_w_gate"]))),
        wup=_tile_up(_padc(f32(inputs["L0_attn_w_up"]))),
        wdn=_tile_down(_padr(f32(inputs["L0_attn_w_down"]))),
        rtw=np.ascontiguousarray(
            f32(inputs["L0_ffn_gate_w"]).reshape(NKC, 128, cfg["E"])
            .transpose(1, 0, 2).reshape(128, NKC * cfg["E"])),
        ebias=np.ascontiguousarray(f32(inputs["L0_ffn_expert_bias"]).reshape(1, -1)),
        sgw=_tile_up(f32(inputs["L0_ffn_shared_gate_w"])),
        suw=_tile_up(f32(inputs["L0_ffn_shared_up_w"])),
        sdw=_tile_down(f32(inputs["L0_ffn_shared_down_w"])),
        egu=np.stack([_tile_up(f32(inputs["L0_ffn_experts_gate_up"])[e])
                      for e in range(cfg["E"])]),
        ed=np.stack([_tile_down(f32(inputs["L0_ffn_experts_down"])[e])
                     for e in range(cfg["E"])]),
    )
    shared = {k: np.ascontiguousarray(v, dtype=np.float32) for k, v in shared.items()}

    nc = build_program(cfg)

    in_maps = []
    for core in range(NCORES):
        b, s = core // nseg, core % nseg
        sel = np.zeros((128, NCORES), dtype=np.float32)
        for j in range(NCORES):
            if j // nseg == b and j % nseg < s:
                sel[:, j] = 1.0
        m = dict(shared)
        m["x"] = np.ascontiguousarray(x[b, s * ntok:(s + 1) * ntok, :])
        m["selmask"] = sel
        in_maps.append(m)

    res = run_bass_kernel_spmd(nc, in_maps, core_ids=list(range(NCORES)))
    out = np.empty((B, T, D), dtype=np.float32)
    for core in range(NCORES):
        b, s = core // nseg, core % nseg
        out[b, s * ntok:(s + 1) * ntok, :] = res.results[core]["out"]
    return out
